# version 03
# baseline (speedup 1.0000x reference)
from contextlib import ExitStack

import numpy as np
import concourse.bass as bass
import concourse.mybir as mybir
from concourse.bass_utils import run_bass_kernel_spmd

F32 = mybir.dt.float32
BF16 = mybir.dt.bfloat16
AF = mybir.ActivationFunctionType
ALU = mybir.AluOpType
AX = mybir.AxisListType

B, S, D = 4, 2048, 4096
NH, HD = 32, 128
HALF = 1024
NT_OWN = HALF // 128
NT_SEQ = S // 128
ND = D // 128
NL = 2048 // 128
NI = 8192 // 128
INT = 8192
LAT = 2048
NE = 8
EPS = 1e-5
CLIP = 8.0
SCALE = HD ** -0.5
NEG = -1.0e30


def _alibi_slopes():
    m = np.arange(1, NH + 1, dtype=np.float64) * (8.0 / NH)
    return (1.0 / np.power(2.0, m)).astype(np.float64)


class C:

    def __init__(self, sem):
        self.sem = sem
        self.n = 0

    def inc(self, inst, kk=1):
        self.n += kk
        inst.then_inc(self.sem, kk)
        return self.n

    def dma(self, inst):
        inst.then_inc(self.sem, 16)
        self.n += 16
        return self.n


def build_nc(dbg=False, phases=99, p2part=99, dbg_hj=None):
    nc = bass.Bass("TRN2", target_bir_lowering=False)
    scratch_kind = "ExternalOutput" if dbg else "Internal"

    x_seq = nc.dram_tensor("x_seq", [S, D], F32, kind="ExternalInput").ap()
    wq = nc.dram_tensor("wq", [D, D], F32, kind="ExternalInput").ap()
    wkv = nc.dram_tensor("wkv", [D, 2 * D], F32, kind="ExternalInput").ap()
    wout = nc.dram_tensor("wout", [D, D], F32, kind="ExternalInput").ap()
    wup = nc.dram_tensor("wup", [D, INT], F32, kind="ExternalInput").ap()
    wdown = nc.dram_tensor("wdown", [INT, D], F32, kind="ExternalInput").ap()
    wfc1 = nc.dram_tensor("wfc1", [D, LAT], F32, kind="ExternalInput").ap()
    wfc2 = nc.dram_tensor("wfc2", [LAT, D], F32, kind="ExternalInput").ap()
    wexp = nc.dram_tensor("wexp", [LAT, LAT], F32, kind="ExternalInput").ap()
    wgate = nc.dram_tensor("wgate", [D, NE], F32, kind="ExternalInput").ap()
    qlnw = nc.dram_tensor("qlnw", [1, D], F32, kind="ExternalInput").ap()
    qlnb = nc.dram_tensor("qlnb", [1, D], F32, kind="ExternalInput").ap()
    klnw = nc.dram_tensor("klnw", [1, D], F32, kind="ExternalInput").ap()
    klnb = nc.dram_tensor("klnb", [1, D], F32, kind="ExternalInput").ap()
    gamma = nc.dram_tensor("gamma", [1, D], F32, kind="ExternalInput").ap()
    beta = nc.dram_tensor("beta", [1, D], F32, kind="ExternalInput").ap()
    kps = nc.dram_tensor("kps", [NH, S], F32, kind="ExternalInput").ap()
    maska = nc.dram_tensor("maska", [1, HALF], F32, kind="ExternalInput").ap()
    tri = nc.dram_tensor("tri", [128, 128], F32, kind="ExternalInput").ap()
    trit = nc.dram_tensor("trit", [128, 128], F32, kind="ExternalInput").ap()
    ident = nc.dram_tensor("ident", [128, 128], F32, kind="ExternalInput").ap()

    out = nc.dram_tensor("out", [HALF, D], F32, kind="ExternalOutput").ap()

    q_s = nc.dram_tensor("q_s", [HALF, D], BF16, kind=scratch_kind).ap()
    k_s = nc.dram_tensor("k_s", [S, D], BF16, kind=scratch_kind).ap()
    v_s = nc.dram_tensor("v_s", [NH, NT_SEQ, 128, 128], BF16, kind=scratch_kind).ap()
    qt_s = nc.dram_tensor("qt_s", [NH, 128, HALF], BF16, kind=scratch_kind).ap()
    kt_s = nc.dram_tensor("kt_s", [NH, 128, S], BF16, kind=scratch_kind).ap()
    proj_s = nc.dram_tensor("proj_s", [HALF, D], F32, kind=scratch_kind).ap()
    xp_s = nc.dram_tensor("xp_s", [HALF, D], F32, kind=scratch_kind).ap()
    eo_s = nc.dram_tensor("eo_s", [NL, 128, HALF], BF16, kind=scratch_kind).ap()
    h_s = nc.dram_tensor("h_s", [NI, 128, HALF], BF16, kind=scratch_kind).ap()
    sh_s = nc.dram_tensor("sh_s", [HALF, D], F32, kind=scratch_kind).ap()
    dbg_ao = None
    dbg_x1t = None
    dbg_p2 = None
    if dbg:
        dbg_ao = nc.dram_tensor("dbg_ao", [HALF, D], BF16, kind="ExternalOutput").ap()
        dbg_x1t = nc.dram_tensor("dbg_x1t", [128, ND * S], BF16, kind="ExternalOutput").ap()
        dbg_p2 = nc.dram_tensor("dbg_p2", [128, S + 16], F32, kind="ExternalOutput").ap()


    perm_rows = [HALF + 128 * t if t < NT_OWN else 128 * (t - NT_OWN) for t in range(NT_SEQ)]

    with ExitStack() as top:
        s_gl = C(top.enter_context(nc.semaphore("s_gl")))
        s_sl = C(top.enter_context(nc.semaphore("s_sl")))
        s_cs = C(top.enter_context(nc.semaphore("s_cs")))
        s_t = C(top.enter_context(nc.semaphore("s_t")))
        s_v = C(top.enter_context(nc.semaphore("s_v")))
        s_s = C(top.enter_context(nc.semaphore("s_s")))
        ctrs = [s_gl, s_sl, s_cs, s_t, s_v, s_s]

        T, V, SC, G, SY = nc.tensor, nc.vector, nc.scalar, nc.gpsimd, nc.sync

        def barrier():
            for eng in (T, V, SC, G, SY):
                for c in ctrs:
                    if c.n:
                        eng.wait_ge(c.sem, c.n)

        idsb = top.enter_context(nc.sbuf_tensor("idsb", [128, 128], BF16))
        tri_sb = top.enter_context(nc.sbuf_tensor("tri_sb", [128, 128], F32))
        trit_b = top.enter_context(nc.sbuf_tensor("trit_b", [128, 128], BF16))
        maska_sb = top.enter_context(nc.sbuf_tensor("maska_sb", [1, HALF], BF16))
        ones1b = top.enter_context(nc.sbuf_tensor("ones1b", [1, 128], BF16))
        ones1f = top.enter_context(nc.sbuf_tensor("ones1f", [1, 128], F32))
        oness = top.enter_context(nc.sbuf_tensor("oness", [1, 128], F32))
        s2_all = top.enter_context(nc.sbuf_tensor("s2_all", [128, NT_OWN], F32))

        live_stacks = []

        def early_return():
            for es in reversed(live_stacks):
                es.close()
            return nc

        s_gl.dma(G.dma_start(out=idsb[:], in_=ident))
        s_sl.dma(SY.dma_start(out=tri_sb[:], in_=tri))
        s_gl.dma(G.dma_start(out=maska_sb[:], in_=maska))
        s_gl.dma(G.dma_start(out=trit_b[:], in_=trit))
        G.memset(ones1b[:], 1.0)
        G.memset(ones1f[:], 1.0)
        ii = G.memset(oness[:], SCALE)
        s_gl.inc(ii, 1)
        barrier()

        def emit_stats(src_ap, ws):
            stt, agg, t1, t2, rr, t3, nmr = ws
            V.wait_ge(s_v.sem, s_v.n)
            for cch in range(8):
                ii = V.bn_stats(stt[:, 6 * cch:6 * cch + 6], src_ap[:, cch * 512:(cch + 1) * 512])
                s_v.inc(ii)
            V.wait_ge(s_v.sem, s_v.n)
            ii = V.bn_aggr(agg[:], stt[:])
            s_v.inc(ii)
            V.wait_ge(s_v.sem, s_v.n)
            ii = V.tensor_scalar_add(t1[:], agg[:, 1:2], EPS)
            s_v.inc(ii)
            V.wait_ge(s_v.sem, s_v.n)
            ii = V.reciprocal(t2[:], t1[:])
            m = s_v.inc(ii)
            SC.wait_ge(s_v.sem, m)
            ii = SC.activation(rr[:], t2[:], AF.Sqrt)
            m2 = s_s.inc(ii)
            V.wait_ge(s_s.sem, m2)
            ii = V.tensor_mul(t3[:], agg[:, 0:1], rr[:])
            s_v.inc(ii)
            V.wait_ge(s_v.sem, s_v.n)
            ii = V.tensor_scalar_mul(nmr[:], t3[:], -1.0)
            return s_v.inc(ii)

        def stats_ws(ctx, pfx):
            return (
                ctx.enter_context(nc.sbuf_tensor(pfx + "stt", [128, 48], F32)),
                ctx.enter_context(nc.sbuf_tensor(pfx + "agg", [128, 2], F32)),
                ctx.enter_context(nc.sbuf_tensor(pfx + "t1", [128, 1], F32)),
                ctx.enter_context(nc.sbuf_tensor(pfx + "t2", [128, 1], F32)),
                ctx.enter_context(nc.sbuf_tensor(pfx + "rr", [128, 1], F32)),
                ctx.enter_context(nc.sbuf_tensor(pfx + "t3", [128, 1], F32)),
                ctx.enter_context(nc.sbuf_tensor(pfx + "nmr", [128, 1], F32)),
            )

        es1 = ExitStack()
        live_stacks.append(es1)
        x1t = es1.enter_context(nc.sbuf_tensor("x1t", [128, ND, S], BF16))

        with ExitStack() as ctx:
            xa = [ctx.enter_context(nc.sbuf_tensor(f"p1xa{i}", [128, D], F32)) for i in range(2)]
            xn = [ctx.enter_context(nc.sbuf_tensor(f"p1xn{i}", [128, D], BF16)) for i in range(2)]
            ws = stats_ws(ctx, "p1")
            rr, nmr = ws[4], ws[6]
            pst = ctx.enter_context(nc.psum_tensor("p1pst", [128, ND * 128], BF16))

            norm_done = []
            tp_t = []
            tp_ev = []
            for t in range(NT_SEQ):
                buf = t % 2
                if t >= 2:
                    SY.wait_ge(s_s.sem, norm_done[t - 2])
                ii = SY.dma_start(out=xa[buf][:], in_=x_seq[t * 128:(t + 1) * 128, :])
                ld = s_sl.dma(ii)
                V.wait_ge(s_sl.sem, ld)
                m = emit_stats(xa[buf], ws)
                SC.wait_ge(s_v.sem, m)
                if t >= 2:
                    SC.wait_ge(s_t.sem, tp_t[t - 2])
                ii = SC.activation(xn[buf][:], xa[buf][:], AF.Identity, bias=nmr[:], scale=rr[:])
                norm_done.append(s_s.inc(ii))
                T.wait_ge(s_s.sem, norm_done[t])
                if t >= 1:
                    T.wait_ge(s_v.sem, tp_ev[t - 1])
                for dch in range(ND):
                    ii = T.transpose(pst[:, dch * 128:(dch + 1) * 128],
                                     xn[buf][:, dch * 128:(dch + 1) * 128], idsb[:])
                tp_t.append(s_t.inc(ii))
                V.wait_ge(s_t.sem, tp_t[t])
                ii = V.tensor_copy(x1t[:, :, t * 128:(t + 1) * 128],
                                   pst[:].rearrange("p (a c) -> p a c", c=128))
                tp_ev.append(s_v.inc(ii))
            if dbg:
                SC.wait_ge(s_v.sem, s_v.n)
                s_cs.dma(SC.dma_start(out=dbg_x1t, in_=x1t[:].rearrange("p a c -> p (a c)")))
            barrier()

        if phases < 2:
            return early_return()

        with ExitStack() as ctx:
            wcol = [ctx.enter_context(nc.sbuf_tensor(f"p1w{i}", [128, ND, 512], BF16)) for i in range(2)]
            ev = [ctx.enter_context(nc.sbuf_tensor(f"p1ev{i}", [128, 512], BF16)) for i in range(4)]
            psA = ctx.enter_context(nc.psum_tensor("p1psA", [128, 2048], F32))

            wq_r = wq.rearrange("(c p) o -> p c o", p=128)
            wkv_r = wkv.rearrange("(c p) o -> p c o", p=128)

            sweeps = []
            for oc in range(D // 512):
                sweeps.append(("q", oc, NT_OWN))
            for oc in range(2 * D // 512):
                sweeps.append(("kv", oc, NT_SEQ))

            base_t = s_t.n
            fin_mark = []
            acc = base_t
            for (_, _, ntl) in sweeps:
                acc += ntl
                fin_mark.append(acc)

            wld = []
            for si, (kind, oc, ntl) in enumerate(sweeps):
                buf = si % 2
                if si >= 2:
                    G.wait_ge(s_t.sem, fin_mark[si - 2])
                src = wq_r[:, :, oc * 512:(oc + 1) * 512] if kind == "q" else \
                    wkv_r[:, :, oc * 512:(oc + 1) * 512]
                wld.append(s_gl.dma(G.dma_start(out=wcol[buf][:], in_=src)))

            ev_done = {}
            st_done = {}
            it = 0
            for si, (kind, oc, ntl) in enumerate(sweeps):
                buf = si % 2
                T.wait_ge(s_gl.sem, wld[si])
                for t in range(ntl):
                    bank = it % 4
                    if bank in ev_done:
                        T.wait_ge(s_v.sem, ev_done[bank])
                    for d in range(ND):
                        ii = T.matmul(psA[:, bank * 512:(bank + 1) * 512],
                                      x1t[:, d, t * 128:(t + 1) * 128],
                                      wcol[buf][:, d, :],
                                      start=(d == 0), stop=(d == ND - 1))
                    mm = s_t.inc(ii)
                    V.wait_ge(s_t.sem, mm)
                    V.wait_ge(s_cs.sem, st_done.get(bank, 0))
                    ii = V.tensor_scalar(ev[bank][:], psA[:, bank * 512:(bank + 1) * 512],
                                         -CLIP, CLIP, ALU.max, ALU.min)
                    ev_done[bank] = s_v.inc(ii)
                    SC.wait_ge(s_v.sem, ev_done[bank])
                    if kind == "q":
                        dst = q_s[t * 128:(t + 1) * 128, oc * 512:(oc + 1) * 512]
                    elif oc < D // 512:
                        r0 = perm_rows[t]
                        dst = k_s[r0:r0 + 128, oc * 512:(oc + 1) * 512]
                    else:
                        c0 = perm_rows[t] // 128
                        h0 = (oc - D // 512) * 4
                        dst = v_s[h0:h0 + 4, c0, :, :].transpose([1, 0, 2])
                        st_done[bank] = s_cs.dma(SC.dma_start(
                            out=dst, in_=ev[bank][:].rearrange("p (a c) -> p a c", c=128)))
                        it += 1
                        continue
                    st_done[bank] = s_cs.dma(SC.dma_start(out=dst, in_=ev[bank][:]))
                    it += 1
            barrier()
        es1.close()
        live_stacks.remove(es1)

        if phases < 3:
            return early_return()

        with ExitStack() as ctx:
            wbt = [ctx.enter_context(nc.sbuf_tensor(f"p2w{i}", [128, D], F32)) for i in range(2)]
            vec1 = ctx.enter_context(nc.sbuf_tensor("p2vec", [1, D], F32))
            qk = [ctx.enter_context(nc.sbuf_tensor(f"p2qk{i}", [128, D], BF16)) for i in range(2)]
            qn = ctx.enter_context(nc.sbuf_tensor("p2qn", [128, D], F32))
            qh = [ctx.enter_context(nc.sbuf_tensor(f"p2qh{i}", [128, D], BF16)) for i in range(2)]
            ws = stats_ws(ctx, "p2")
            rr, nmr = ws[4], ws[6]
            psW = ctx.enter_context(nc.psum_tensor("p2psW", [128, 2048], F32))
            pst = ctx.enter_context(nc.psum_tensor("p2pst", [128, ND * 128], BF16))

            p2_gi = 0
            p2_norm = []
            p2_tpev = []
            p2_st = []
            whiches = ("q", "k") if p2part >= 7 else ("q",)
            for which in whiches:
                wv, bv = (qlnw, qlnb) if which == "q" else (klnw, klnb)
                lhs1 = oness if which == "q" else ones1f
                for wi, vecap in enumerate((wv, bv)):
                    SY.wait_ge(s_v.sem, s_v.n)
                    ld = s_sl.dma(SY.dma_start(out=vec1[:], in_=vecap))
                    T.wait_ge(s_sl.sem, ld)
                    T.wait_ge(s_v.sem, s_v.n)
                    for hhf in range(2):
                        for cch in range(4):
                            ii = T.matmul(psW[:, cch * 512:(cch + 1) * 512], lhs1[:],
                                          vec1[:, hhf * 2048 + cch * 512:hhf * 2048 + (cch + 1) * 512],
                                          start=True, stop=True)
                        m = s_t.inc(ii)
                        V.wait_ge(s_t.sem, m)
                        V.wait_ge(s_v.sem, s_v.n)
                        ii = V.tensor_copy(wbt[wi][:, hhf * 2048:(hhf + 1) * 2048], psW[:])
                        s_v.inc(ii)
                        T.wait_ge(s_v.sem, s_v.n)

                ntile = NT_OWN if which == "q" else NT_SEQ
                if p2part < 2:
                    ntile = 0
                for t in range(ntile):
                    buf = p2_gi % 2
                    if p2_gi >= 2:
                        SY.wait_ge(s_v.sem, p2_norm[p2_gi - 2])
                    src = q_s[t * 128:(t + 1) * 128, :] if which == "q" else k_s[t * 128:(t + 1) * 128, :]
                    ld = s_sl.dma(SY.dma_start(out=qk[buf][:], in_=src))
                    V.wait_ge(s_sl.sem, ld)
                    m = emit_stats(qk[buf], ws)
                    nd = m
                    if p2part >= 3:
                        SC.wait_ge(s_v.sem, m)
                        ii = SC.activation(qn[:], qk[buf][:], AF.Identity, bias=nmr[:], scale=rr[:])
                        sm = s_s.inc(ii)
                    if p2part >= 4:
                        V.wait_ge(s_s.sem, sm)
                        if p2_gi >= 2:
                            V.wait_ge(s_cs.sem, p2_st[p2_gi - 2])
                        ii = V.tensor_mul(qn[:], qn[:], wbt[0][:])
                        s_v.inc(ii)
                        V.wait_ge(s_v.sem, s_v.n)
                        ii = V.tensor_add(qh[buf][:], qn[:], wbt[1][:])
                        nd = s_v.inc(ii)
                    p2_norm.append(nd)
                    if p2part >= 5:
                        T.wait_ge(s_v.sem, nd)
                        if p2_tpev:
                            T.wait_ge(s_s.sem, p2_tpev[-1])
                        for dch in range(ND):
                            ii = T.transpose(pst[:, dch * 128:(dch + 1) * 128],
                                             qh[buf][:, dch * 128:(dch + 1) * 128], idsb[:])
                        tpt = s_t.inc(ii)
                        SC.wait_ge(s_t.sem, tpt)
                        ii = SC.copy(qh[buf][:], pst[:])
                        te = s_s.inc(ii)
                        p2_tpev.append(te)
                        SC.wait_ge(s_s.sem, te)
                    if p2part >= 6:
                        for hh in range(NH):
                            if which == "q":
                                dst = qt_s[hh, :, t * 128:(t + 1) * 128]
                            else:
                                dst = kt_s[hh, :, t * 128:(t + 1) * 128]
                            stm = s_cs.dma(
                                SC.dma_start(out=dst, in_=qh[buf][:, hh * 128:(hh + 1) * 128]))
                        p2_st.append(stm)
                    p2_gi += 1
            barrier()

        if phases < 4:
            return early_return()

        es2 = ExitStack()
        live_stacks.append(es2)
        ao = es2.enter_context(nc.sbuf_tensor("ao", [128, NT_OWN, D], BF16))
        with ExitStack() as ctx:
            kt = [ctx.enter_context(nc.sbuf_tensor(f"p3kt{i}", [128, S], BF16)) for i in range(2)]
            vh = [ctx.enter_context(nc.sbuf_tensor(f"p3vh{i}", [128, NT_SEQ, 128], BF16)) for i in range(2)]
            qt = [ctx.enter_context(nc.sbuf_tensor(f"p3qt{i}", [128, HALF], BF16)) for i in range(2)]
            kr = [ctx.enter_context(nc.sbuf_tensor(f"p3kr{i}", [1, S], F32)) for i in range(2)]
            bias2 = [ctx.enter_context(nc.sbuf_tensor(f"p3bias{i}", [128, S], F32)) for i in range(2)]
            ssb2 = [ctx.enter_context(nc.sbuf_tensor(f"p3ssb{i}", [128, S], F32)) for i in range(2)]
            pb2 = [ctx.enter_context(nc.sbuf_tensor(f"p3pb{i}", [128, S], BF16)) for i in range(2)]
            ptb = ctx.enter_context(nc.sbuf_tensor("p3ptb", [128, S], BF16))
            negm2 = [ctx.enter_context(nc.sbuf_tensor(f"p3negm{i}", [128, 1], F32)) for i in range(2)]
            lsum2 = [ctx.enter_context(nc.sbuf_tensor(f"p3lsum{i}", [128, 1], F32)) for i in range(2)]
            rl2 = [ctx.enter_context(nc.sbuf_tensor(f"p3rl{i}", [128, 1], F32)) for i in range(2)]
            psS = ctx.enter_context(nc.psum_tensor("p3psS", [128, 2048], F32))
            psT = ctx.enter_context(nc.psum_tensor("p3psT", [128, 2048], BF16))
            psV = ctx.enter_context(nc.psum_tensor("p3psV", [128, 256], F32))
            psB1 = ctx.enter_context(nc.psum_tensor("p3psB1", [128, 512], F32))


            def load_head(hh, guard_mark):
                buf = hh % 2
                if guard_mark:
                    SY.wait_ge(s_t.sem, guard_mark)
                s_sl.dma(SY.dma_start(out=kt[buf][:], in_=kt_s[hh]))
                s_sl.dma(SY.dma_start(out=qt[buf][:], in_=qt_s[hh]))
                s_sl.dma(SY.dma_start(out=vh[buf][:], in_=v_s[hh].transpose([1, 0, 2])))
                return s_sl.dma(SY.dma_start(out=kr[buf][:], in_=kps[hh:hh + 1, :]))

            rl4 = [ctx.enter_context(nc.sbuf_tensor(f"p3rl4_{i}", [128, 1], F32)) for i in range(4)]

            iters = [(hh, j) for hh in range(NH) for j in range(NT_OWN)]
            NIT = len(iters)
            sc_mark = [0] * NIT
            tt_mark = [0] * NIT
            mx_mark = [0] * NIT
            ex_mark = [0] * NIT
            tpm = [0] * NIT
            cp_mark = [0] * NIT
            pv_mark = [0] * NIT
            ev_mark = [0] * NIT
            rl_mark = [0] * NIT
            last_pv = [0] * NH
            bias_built = [0] * (NH + 1)
            bias_ev = [0]
            head_ld = {0: load_head(0, 0)}

            def LJ(i):
                return HALF + 128 * (iters[i][1] + 1)

            def build_bias_chunk(hh_next, cch, guard_tt):
                nbuf = hh_next % 2
                T.wait_ge(s_sl.sem, head_ld[hh_next])
                if bias_ev[0]:
                    T.wait_ge(s_v.sem, bias_ev[0])
                ii = T.matmul(psB1[:], ones1f[:],
                              kr[nbuf][:, cch * 512:(cch + 1) * 512], start=True, stop=True)
                m = s_t.inc(ii)
                V.wait_ge(s_t.sem, m)
                if guard_tt:
                    V.wait_ge(s_v.sem, guard_tt)
                ii = V.tensor_copy(bias2[nbuf][:, cch * 512:(cch + 1) * 512], psB1[:])
                bias_ev[0] = s_v.inc(ii)
                return bias_ev[0]

            for cch in range(4):
                bias_built[0] = build_bias_chunk(0, cch, 0)

            for i in range(NIT + 3):
                if i < NIT:
                    hh, j = iters[i]
                    L = LJ(i)
                    buf = hh % 2
                    if j == 0:
                        T.wait_ge(s_v.sem, bias_built[hh])
                        T.wait_ge(s_sl.sem, head_ld[hh])
                    if i >= 1:
                        T.wait_ge(s_v.sem, tt_mark[i - 1])
                    lastbank = ((L - 1) // 512) * 512
                    for c0 in range(0, L, 512):
                        w = min(512, L - c0)
                        ii = T.matmul(psS[:, c0:c0 + w],
                                      qt[buf][:, j * 128:(j + 1) * 128],
                                      kt[buf][:, c0:c0 + w],
                                      start=True, stop=(c0 != lastbank))
                    ii = T.matmul(psS[:, L - 128:L], trit_b[:], idsb[:],
                                  start=False, stop=True)
                    sc_mark[i] = s_t.inc(ii)
                    if j == 3 and hh + 1 < NH:
                        head_ld[hh + 1] = load_head(hh + 1, last_pv[hh - 1] if hh >= 1 else 0)
                    if 4 <= j <= 7 and hh + 1 < NH:
                        bias_built[hh + 1] = build_bias_chunk(
                            hh + 1, j - 4, tt_mark[i] if j == 4 else 0)
                if i >= 2 and i - 2 < NIT:
                    k = i - 2
                    nck2 = LJ(k) // 128
                    T.wait_ge(s_s.sem, ex_mark[k])
                    if k >= 1:
                        T.wait_ge(s_s.sem, cp_mark[k - 1])
                    for c in range(nck2):
                        ii = T.transpose(psT[:, c * 128:(c + 1) * 128],
                                         pb2[k % 2][:, c * 128:(c + 1) * 128], idsb[:])
                    tpm[k] = s_t.inc(ii)
                if i >= 3 and i - 3 < NIT:
                    k = i - 3
                    hh3, j3 = iters[k]
                    nck3 = LJ(k) // 128
                    slot = k % 2
                    T.wait_ge(s_s.sem, cp_mark[k])
                    if k >= 2:
                        T.wait_ge(s_s.sem, ev_mark[k - 2])
                    for c in range(nck3):
                        ii = T.matmul(psV[:, slot * 128:(slot + 1) * 128],
                                      ptb[:, c * 128:(c + 1) * 128],
                                      vh[hh3 % 2][:, c, :],
                                      start=(c == 0), stop=(c == nck3 - 1))
                    pv_mark[k] = s_t.inc(ii)
                    if j3 == NT_OWN - 1:
                        last_pv[hh3] = pv_mark[k]
                if i >= 1 and i - 1 < NIT:
                    k = i - 1
                    Lk = LJ(k)
                    SC.wait_ge(s_v.sem, mx_mark[k])
                    if k >= 2:
                        SC.wait_ge(s_t.sem, tpm[k - 2])
                    ii = SC.activation(pb2[k % 2][:, 0:Lk], ssb2[k % 2][:, 0:Lk], AF.Exp,
                                       bias=negm2[k % 2][:], scale=1.0,
                                       accum_out=lsum2[k % 2][:])
                    ex_mark[k] = s_s.inc(ii)
                if i >= 2 and i - 2 < NIT:
                    k = i - 2
                    nckk = LJ(k) // 128
                    SC.wait_ge(s_t.sem, tpm[k])
                    if k >= 1:
                        SC.wait_ge(s_t.sem, pv_mark[k - 1])
                    ii = SC.copy(ptb[:, 0:nckk * 128], psT[:, 0:nckk * 128])
                    cp_mark[k] = s_s.inc(ii)
                if i >= 3 and i - 3 < NIT:
                    k = i - 3
                    hk, jk = iters[k]
                    slot = k % 2
                    SC.wait_ge(s_t.sem, pv_mark[k])
                    SC.wait_ge(s_v.sem, rl_mark[k])
                    ii = SC.activation(ao[:, jk, hk * 128:(hk + 1) * 128],
                                       psV[:, slot * 128:(slot + 1) * 128],
                                       AF.Identity, bias=0.0, scale=rl4[k % 4][:])
                    ev_mark[k] = s_s.inc(ii)
                if i < NIT:
                    hh, j = iters[i]
                    L = LJ(i)
                    V.wait_ge(s_t.sem, sc_mark[i])
                    if i >= 2:
                        V.wait_ge(s_s.sem, ex_mark[i - 2])
                    ii = V.tensor_add(ssb2[i % 2][:, 0:L], psS[:, 0:L], bias2[hh % 2][:, 0:L])
                    tt_mark[i] = s_v.inc(ii)
                    V.wait_ge(s_v.sem, tt_mark[i])
                    ii = V.reduce_max(negm2[i % 2][:], ssb2[i % 2][:, 0:L],
                                      axis=AX.X, negate=True)
                    mx_mark[i] = s_v.inc(ii)
                if i >= 1 and i - 1 < NIT:
                    k = i - 1
                    V.wait_ge(s_s.sem, ex_mark[k])
                    ii = V.reciprocal(rl4[k % 4][:], lsum2[k % 2][:])
                    rl_mark[k] = s_v.inc(ii)
            if dbg:
                SC.wait_ge(s_v.sem, s_v.n)
                SC.wait_ge(s_s.sem, s_s.n)
                SC.wait_ge(s_t.sem, s_t.n)
                G.wait_ge(s_s.sem, s_s.n)
                G.wait_ge(s_v.sem, s_v.n)
                s_gl.dma(G.dma_start(out=dbg_p2[:, 0:S], in_=pb2[(NIT - 1) % 2][:]))
                s_cs.dma(SC.dma_start(out=dbg_p2[:, S:S + 1], in_=lsum2[(NIT - 1) % 2][:], allow_slow_non_contiguous=True))
                s_cs.dma(SC.dma_start(out=dbg_p2[:, S + 1:S + 2], in_=rl4[(NIT - 1) % 4][:], allow_slow_non_contiguous=True))
                s_cs.dma(SC.dma_start(out=dbg_p2[:, S + 2:S + 3], in_=negm2[(NIT - 1) % 2][:], allow_slow_non_contiguous=True))
                for j in range(NT_OWN):
                    s_cs.dma(SC.dma_start(out=dbg_ao[j * 128:(j + 1) * 128, :], in_=ao[:, j, :]))
            barrier()

        if phases < 5:
            return early_return()

        es3 = ExitStack()
        live_stacks.append(es3)
        aot = es3.enter_context(nc.sbuf_tensor("aot", [128, ND, HALF], BF16, side="right"))
        with ExitStack() as ctx:
            pst = ctx.enter_context(nc.psum_tensor("p4pst", [128, ND * 128], BF16))
            for j in range(NT_OWN):
                if j >= 1:
                    T.wait_ge(s_v.sem, s_v.n)
                for dch in range(ND):
                    ii = T.transpose(pst[:, dch * 128:(dch + 1) * 128],
                                     ao[:, j, dch * 128:(dch + 1) * 128], idsb[:])
                m = s_t.inc(ii)
                V.wait_ge(s_t.sem, m)
                ii = V.tensor_copy(aot[:, :, j * 128:(j + 1) * 128],
                                   pst[:].rearrange("p (a c) -> p a c", c=128))
                s_v.inc(ii)
            barrier()
        es2.close()
        live_stacks.remove(es2)

        with ExitStack() as ctx:
            wcol = [ctx.enter_context(nc.sbuf_tensor(f"p4w{i}", [128, ND, 512], BF16)) for i in range(2)]
            ev = [ctx.enter_context(nc.sbuf_tensor(f"p4ev{i}", [128, 512], F32)) for i in range(4)]
            psA = ctx.enter_context(nc.psum_tensor("p4psA", [128, 2048], F32))
            wout_r = wout.rearrange("(c p) o -> p c o", p=128)
            nsw = D // 512
            base_t = s_t.n
            fin_mark = [base_t + (oc + 1) * NT_OWN for oc in range(nsw)]
            wld = []
            for oc in range(nsw):
                buf = oc % 2
                if oc >= 2:
                    G.wait_ge(s_t.sem, fin_mark[oc - 2])
                wld.append(s_gl.dma(G.dma_start(out=wcol[buf][:],
                                                in_=wout_r[:, :, oc * 512:(oc + 1) * 512])))
            ev_done = {}
            st_done = {}
            for oc in range(nsw):
                buf = oc % 2
                T.wait_ge(s_gl.sem, wld[oc])
                for t in range(NT_OWN):
                    bank = (oc * NT_OWN + t) % 4
                    if bank in ev_done:
                        T.wait_ge(s_v.sem, ev_done[bank])
                    for d in range(ND):
                        ii = T.matmul(psA[:, bank * 512:(bank + 1) * 512],
                                      aot[:, d, t * 128:(t + 1) * 128],
                                      wcol[buf][:, d, :],
                                      start=(d == 0), stop=(d == ND - 1))
                    mm = s_t.inc(ii)
                    V.wait_ge(s_t.sem, mm)
                    V.wait_ge(s_cs.sem, st_done.get(bank, 0))
                    ii = V.tensor_copy(ev[bank][:], psA[:, bank * 512:(bank + 1) * 512])
                    ev_done[bank] = s_v.inc(ii)
                    SC.wait_ge(s_v.sem, ev_done[bank])
                    st_done[bank] = s_cs.dma(
                        SC.dma_start(out=proj_s[t * 128:(t + 1) * 128, oc * 512:(oc + 1) * 512],
                                     in_=ev[bank][:]))
            barrier()
        es3.close()
        live_stacks.remove(es3)

        if phases < 6:
            return early_return()

        es4 = ExitStack()
        live_stacks.append(es4)
        x2t = es4.enter_context(nc.sbuf_tensor("x2t", [128, ND, HALF], BF16))
        with ExitStack() as ctx:
            pr = [ctx.enter_context(nc.sbuf_tensor(f"p5pr{i}", [128, D], F32)) for i in range(2)]
            xt_ = [ctx.enter_context(nc.sbuf_tensor(f"p5xt{i}", [128, D], F32)) for i in range(2)]
            x2b = [ctx.enter_context(nc.sbuf_tensor(f"p5x2{i}", [128, D], BF16)) for i in range(2)]
            ws = stats_ws(ctx, "p5")
            rr, nmr = ws[4], ws[6]
            pst = ctx.enter_context(nc.psum_tensor("p5pst", [128, ND * 128], BF16))

            tp_t = [0] * NT_OWN
            st_mark = [0] * NT_OWN
            for t in range(NT_OWN):
                buf = t % 2
                if t >= 2:
                    SY.wait_ge(s_t.sem, tp_t[t - 2])
                    SY.wait_ge(s_cs.sem, st_mark[t - 2])
                ld = s_sl.dma(SY.dma_start(out=pr[buf][:], in_=proj_s[t * 128:(t + 1) * 128, :]))
                ld = s_sl.dma(SY.dma_start(out=xt_[buf][:], in_=x_seq[t * 128:(t + 1) * 128, :]))
                V.wait_ge(s_sl.sem, ld)
                m = emit_stats(pr[buf], ws)
                SC.wait_ge(s_v.sem, m)
                ii = SC.activation(pr[buf][:], pr[buf][:], AF.Identity, bias=nmr[:], scale=rr[:])
                m = s_s.inc(ii)
                V.wait_ge(s_s.sem, m)
                ii = V.tensor_add(xt_[buf][:], xt_[buf][:], pr[buf][:])
                xp_m = s_v.inc(ii)
                SC.wait_ge(s_v.sem, xp_m)
                st_mark[t] = s_cs.dma(SC.dma_start(out=xp_s[t * 128:(t + 1) * 128, :], in_=xt_[buf][:]))
                m = emit_stats(xt_[buf], ws)
                SC.wait_ge(s_v.sem, m)
                ii = SC.activation(x2b[buf][:], xt_[buf][:], AF.Identity, bias=nmr[:], scale=rr[:])
                x2_m = s_s.inc(ii)
                T.wait_ge(s_s.sem, x2_m)
                if t >= 1:
                    T.wait_ge(s_v.sem, s_v.n)
                for dch in range(ND):
                    ii = T.transpose(pst[:, dch * 128:(dch + 1) * 128],
                                     x2b[buf][:, dch * 128:(dch + 1) * 128], idsb[:])
                tp_t[t] = s_t.inc(ii)
                V.wait_ge(s_t.sem, tp_t[t])
                ii = V.tensor_copy(x2t[:, :, t * 128:(t + 1) * 128],
                                   pst[:].rearrange("p (a c) -> p a c", c=128))
                s_v.inc(ii)
            barrier()

        if phases < 7:
            return early_return()

        with ExitStack() as ctx:
            gw = ctx.enter_context(nc.sbuf_tensor("p6gw", [128, ND, NE], BF16))
            lg = ctx.enter_context(nc.sbuf_tensor("p6lg", [128, NE], F32))
            pw = ctx.enter_context(nc.sbuf_tensor("p6pw", [128, 7, 7], F32))
            mk = ctx.enter_context(nc.sbuf_tensor("p6mk", [128, NE - 1], F32))
            psA = ctx.enter_context(nc.psum_tensor("p6psA", [128, 512], F32))
            s_gl.dma(G.dma_start(out=gw[:], in_=wgate.rearrange("(c p) o -> p c o", p=128)))
            T.wait_ge(s_gl.sem, s_gl.n)
            for t in range(NT_OWN):
                if t >= 1:
                    T.wait_ge(s_v.sem, s_v.n)
                for d in range(ND):
                    ii = T.matmul(psA[:, 0:NE], x2t[:, d, t * 128:(t + 1) * 128], gw[:, d, :],
                                  start=(d == 0), stop=(d == ND - 1))
                m = s_t.inc(ii)
                V.wait_ge(s_t.sem, m)
                ii = V.tensor_copy(lg[:], psA[:, 0:NE])
                s_v.inc(ii)
                V.wait_ge(s_v.sem, s_v.n)
                for kk in range(1, NE):
                    ii = V.tensor_add(pw[:, kk - 1, 0:NE - kk], lg[:, 0:NE - kk], lg[:, kk:NE])
                    s_v.inc(ii)
                V.wait_ge(s_v.sem, s_v.n)
                for kk in range(1, NE):
                    ii = V.reduce_max(mk[:, kk - 1:kk], pw[:, kk - 1, 0:NE - kk], axis=AX.X)
                    s_v.inc(ii)
                V.wait_ge(s_v.sem, s_v.n)
                ii = V.reduce_max(s2_all[:, t:t + 1], mk[:], axis=AX.X)
                s_v.inc(ii)
            barrier()

        es5 = ExitStack()
        live_stacks.append(es5)
        xlat = es5.enter_context(nc.sbuf_tensor("xlat", [128, NL, HALF], BF16, side="right"))
        with ExitStack() as ctx:
            wt_ = [ctx.enter_context(nc.sbuf_tensor(f"p7w{i}", [128, ND, 128], BF16)) for i in range(2)]
            psA = ctx.enter_context(nc.psum_tensor("p7psA", [128, 2048], F32))
            wfc1_r = wfc1.rearrange("(c p) o -> p c o", p=128)
            base_t = s_t.n
            fin_mark = [base_t + (lc + 1) * 2 for lc in range(NL)]
            wld = []
            for lc in range(NL):
                buf = lc % 2
                if lc >= 2:
                    G.wait_ge(s_t.sem, fin_mark[lc - 2])
                wld.append(s_gl.dma(G.dma_start(out=wt_[buf][:],
                                                in_=wfc1_r[:, :, lc * 128:(lc + 1) * 128])))
            evd = {}
            for lc in range(NL):
                buf = lc % 2
                T.wait_ge(s_gl.sem, wld[lc])
                for tb in range(2):
                    bank = (lc * 2 + tb) % 4
                    if bank in evd:
                        T.wait_ge(s_v.sem, evd[bank])
                    for d in range(ND):
                        ii = T.matmul(psA[:, bank * 512:(bank + 1) * 512],
                                      wt_[buf][:, d, :],
                                      x2t[:, d, tb * 512:(tb + 1) * 512],
                                      start=(d == 0), stop=(d == ND - 1))
                    mm = s_t.inc(ii)
                    V.wait_ge(s_t.sem, mm)
                    ii = V.tensor_copy(xlat[:, lc, tb * 512:(tb + 1) * 512],
                                       psA[:, bank * 512:(bank + 1) * 512])
                    evd[bank] = s_v.inc(ii)
            barrier()

        es6 = ExitStack()
        live_stacks.append(es6)
        eo = es6.enter_context(nc.sbuf_tensor("eo", [128, NL, HALF], BF16, side="right"))
        with ExitStack() as ctx:
            wt_ = [ctx.enter_context(nc.sbuf_tensor(f"p8w{i}", [128, NL, 128], BF16)) for i in range(2)]
            psA = ctx.enter_context(nc.psum_tensor("p8psA", [128, 2048], F32))
            wexp_r = wexp.rearrange("(c p) o -> p c o", p=128)
            base_t = s_t.n
            fin_mark = [base_t + (mc + 1) * 2 for mc in range(NL)]
            wld = []
            for mc in range(NL):
                buf = mc % 2
                if mc >= 2:
                    G.wait_ge(s_t.sem, fin_mark[mc - 2])
                wld.append(s_gl.dma(G.dma_start(out=wt_[buf][:],
                                                in_=wexp_r[:, :, mc * 128:(mc + 1) * 128])))
            evd = {}
            for mc in range(NL):
                buf = mc % 2
                T.wait_ge(s_gl.sem, wld[mc])
                for tb in range(2):
                    bank = (mc * 2 + tb) % 4
                    if bank in evd:
                        T.wait_ge(s_v.sem, evd[bank])
                    for lc in range(NL):
                        ii = T.matmul(psA[:, bank * 512:(bank + 1) * 512],
                                      wt_[buf][:, lc, :],
                                      xlat[:, lc, tb * 512:(tb + 1) * 512],
                                      start=(lc == 0), stop=(lc == NL - 1))
                    mm = s_t.inc(ii)
                    V.wait_ge(s_t.sem, mm)
                    ii = V.tensor_copy(eo[:, mc, tb * 512:(tb + 1) * 512],
                                       psA[:, bank * 512:(bank + 1) * 512])
                    evd[bank] = s_v.inc(ii)
            SC.wait_ge(s_v.sem, s_v.n)
            s_cs.dma(SC.dma_start(out=eo_s.rearrange("a p c -> p a c"), in_=eo[:]))
            barrier()
        es6.close()
        live_stacks.remove(es6)
        es5.close()
        live_stacks.remove(es5)

        if phases < 8:
            return early_return()

        with ExitStack() as ctx:
            wt_ = [ctx.enter_context(nc.sbuf_tensor(f"p9w{i}", [128, ND, 128], BF16)) for i in range(2)]
            htmp = [ctx.enter_context(nc.sbuf_tensor(f"p9h{i}", [128, 512], F32)) for i in range(2)]
            hb = [ctx.enter_context(nc.sbuf_tensor(f"p9hb{i}", [128, 512], BF16)) for i in range(4)]
            psA = ctx.enter_context(nc.psum_tensor("p9psA", [128, 2048], F32))
            wup_r = wup.rearrange("(c p) o -> p c o", p=128)
            base_t = s_t.n
            fin_mark = [base_t + (ic + 1) * 2 for ic in range(NI)]
            wld = []
            for ic in range(NI):
                buf = ic % 2
                if ic >= 2:
                    G.wait_ge(s_t.sem, fin_mark[ic - 2])
                wld.append(s_gl.dma(G.dma_start(out=wt_[buf][:],
                                                in_=wup_r[:, :, ic * 128:(ic + 1) * 128])))
            evd = {}
            st_done = {}
            for ic in range(NI):
                buf = ic % 2
                T.wait_ge(s_gl.sem, wld[ic])
                for tb in range(2):
                    it2 = ic * 2 + tb
                    bank = it2 % 4
                    if bank in evd:
                        T.wait_ge(s_s.sem, evd[bank])
                    for d in range(ND):
                        ii = T.matmul(psA[:, bank * 512:(bank + 1) * 512],
                                      wt_[buf][:, d, :],
                                      x2t[:, d, tb * 512:(tb + 1) * 512],
                                      start=(d == 0), stop=(d == ND - 1))
                    mm = s_t.inc(ii)
                    ebuf = it2 % 4
                    SC.wait_ge(s_t.sem, mm)
                    SC.wait_ge(s_cs.sem, st_done.get(ebuf, 0))
                    ii = SC.activation(htmp[it2 % 2][:], psA[:, bank * 512:(bank + 1) * 512], AF.Relu)
                    evd[bank] = s_s.inc(ii)
                    SC.wait_ge(s_s.sem, evd[bank])
                    ii = SC.activation(hb[ebuf][:], htmp[it2 % 2][:], AF.Square)
                    m = s_s.inc(ii)
                    SC.wait_ge(s_s.sem, m)
                    st_done[ebuf] = s_cs.dma(
                        SC.dma_start(out=h_s[ic, :, tb * 512:(tb + 1) * 512], in_=hb[ebuf][:]))
            barrier()
        es4.close()
        live_stacks.remove(es4)

        if phases < 9:
            return early_return()

        with ExitStack() as ctx:
            hh_ = ctx.enter_context(nc.sbuf_tensor("p10hh", [128, NI, 512], BF16))
            wcol = [ctx.enter_context(nc.sbuf_tensor(f"p10w{i}", [128, NI, 512], BF16)) for i in range(2)]
            ev = [ctx.enter_context(nc.sbuf_tensor(f"p10ev{i}", [128, 512], F32)) for i in range(4)]
            psA = ctx.enter_context(nc.psum_tensor("p10psA", [128, 2048], F32))
            wdown_r = wdown.rearrange("(c p) o -> p c o", p=128)
            h_r = h_s.rearrange("a p c -> p a c")
            for th in range(2):
                SY.wait_ge(s_t.sem, s_t.n)
                h_ld = s_sl.dma(SY.dma_start(out=hh_[:], in_=h_r[:, :, th * 512:(th + 1) * 512]))
                base_t = s_t.n
                fin_mark = [base_t + (oc + 1) * 4 for oc in range(8)]
                wld = []
                for oc in range(8):
                    buf = oc % 2
                    if oc >= 2:
                        G.wait_ge(s_t.sem, fin_mark[oc - 2])
                    wld.append(s_gl.dma(G.dma_start(out=wcol[buf][:],
                                                    in_=wdown_r[:, :, oc * 512:(oc + 1) * 512])))
                evd = {}
                st_done = {}
                for oc in range(8):
                    buf = oc % 2
                    T.wait_ge(s_gl.sem, wld[oc])
                    T.wait_ge(s_sl.sem, h_ld)
                    for t in range(4):
                        bank = (oc * 4 + t) % 4
                        if bank in evd:
                            T.wait_ge(s_v.sem, evd[bank])
                        for i in range(NI):
                            ii = T.matmul(psA[:, bank * 512:(bank + 1) * 512],
                                          hh_[:, i, t * 128:(t + 1) * 128],
                                          wcol[buf][:, i, :],
                                          start=(i == 0), stop=(i == NI - 1))
                        mm = s_t.inc(ii)
                        V.wait_ge(s_t.sem, mm)
                        V.wait_ge(s_cs.sem, st_done.get(bank, 0))
                        ii = V.tensor_copy(ev[bank][:], psA[:, bank * 512:(bank + 1) * 512])
                        evd[bank] = s_v.inc(ii)
                        SC.wait_ge(s_v.sem, evd[bank])
                        r0 = th * 512 + t * 128
                        st_done[bank] = s_cs.dma(
                            SC.dma_start(out=sh_s[r0:r0 + 128, oc * 512:(oc + 1) * 512],
                                         in_=ev[bank][:]))
            barrier()

        if phases < 10:
            return early_return()

        with ExitStack() as ctx:
            wf2 = ctx.enter_context(nc.sbuf_tensor("p11w", [128, NL, D], BF16))
            eot = [ctx.enter_context(nc.sbuf_tensor("p11e0", [128, NL, 128], BF16))]
            shx = ctx.enter_context(nc.sbuf_tensor("p11sx", [128, D], F32))
            z = ctx.enter_context(nc.sbuf_tensor("p11z", [128, D], F32))
            gb = ctx.enter_context(nc.sbuf_tensor("p11gb", [128, D], F32))
            bb = ctx.enter_context(nc.sbuf_tensor("p11bb", [128, D], F32))
            vec1 = ctx.enter_context(nc.sbuf_tensor("p11vec", [1, 512], F32))
            ws = stats_ws(ctx, "p11")
            rr, nmr = ws[4], ws[6]
            psA = ctx.enter_context(nc.psum_tensor("p11psA", [128, 2048], F32))
            psB = ctx.enter_context(nc.psum_tensor("p11psB", [128, 2048], F32))

            s_gl.dma(G.dma_start(out=wf2[:], in_=wfc2.rearrange("(c p) o -> p c o", p=128)))

            for wi, vecap in enumerate((gamma, beta)):
                dstt = gb if wi == 0 else bb
                for cch in range(8):
                    T.wait_ge(s_v.sem, s_v.n)
                    SY.wait_ge(s_t.sem, s_t.n)
                    ld = s_sl.dma(SY.dma_start(out=vec1[:], in_=vecap[0:1, cch * 512:(cch + 1) * 512]))
                    T.wait_ge(s_sl.sem, ld)
                    ii = T.matmul(psA[:, 0:512], ones1f[:], vec1[:], start=True, stop=True)
                    m = s_t.inc(ii)
                    V.wait_ge(s_t.sem, m)
                    ii = V.tensor_copy(dstt[:, cch * 512:(cch + 1) * 512], psA[:, 0:512])
                    s_v.inc(ii)

            T.wait_ge(s_gl.sem, s_gl.n)
            zev = 0
            for t in range(NT_OWN):
                buf = 0
                SY.wait_ge(s_v.sem, s_v.n)
                s_sl.dma(SY.dma_start(out=eot[buf][:],
                                      in_=eo_s.rearrange("a p c -> p a c")[:, :, t * 128:(t + 1) * 128]))
                ld = s_sl.dma(SY.dma_start(out=shx[:], in_=sh_s[t * 128:(t + 1) * 128, :]))
                T.wait_ge(s_sl.sem, ld)
                if zev:
                    T.wait_ge(s_v.sem, zev)
                for mc in range(NL):
                    for o in range(8):
                        pbank = psA if o < 4 else psB
                        ii = T.matmul(pbank[:, (o % 4) * 512:((o % 4) + 1) * 512],
                                      eot[buf][:, mc, :],
                                      wf2[:, mc, o * 512:(o + 1) * 512],
                                      start=(mc == 0), stop=(mc == NL - 1))
                mm = s_t.inc(ii)
                V.wait_ge(s_t.sem, mm)
                for o in range(8):
                    pbank = psA if o < 4 else psB
                    ii = V.tensor_scalar(z[:, o * 512:(o + 1) * 512],
                                         pbank[:, (o % 4) * 512:((o % 4) + 1) * 512],
                                         s2_all[:, t:t + 1], None, ALU.mult)
                    zev = s_v.inc(ii)
                V.wait_ge(s_v.sem, zev)
                V.wait_ge(s_sl.sem, ld)
                ii = V.tensor_add(z[:], z[:], shx[:])
                m_shc = s_v.inc(ii)
                SY.wait_ge(s_v.sem, m_shc)
                ld2 = s_sl.dma(SY.dma_start(out=shx[:], in_=xp_s[t * 128:(t + 1) * 128, :]))
                m = emit_stats(z, ws)
                SC.wait_ge(s_v.sem, m)
                ii = SC.activation(z[:], z[:], AF.Identity, bias=nmr[:], scale=rr[:])
                m = s_s.inc(ii)
                V.wait_ge(s_s.sem, m)
                ii = V.tensor_mul(z[:], z[:], gb[:])
                s_v.inc(ii)
                V.wait_ge(s_v.sem, s_v.n)
                ii = V.tensor_add(z[:], z[:], bb[:])
                s_v.inc(ii)
                m = emit_stats(z, ws)
                SC.wait_ge(s_v.sem, m)
                ii = SC.activation(z[:], z[:], AF.Identity, bias=nmr[:], scale=rr[:])
                m = s_s.inc(ii)
                V.wait_ge(s_s.sem, m)
                V.wait_ge(s_sl.sem, ld2)
                ii = V.tensor_add(z[:], z[:], shx[:])
                fin = s_v.inc(ii)
                SC.wait_ge(s_v.sem, fin)
                st = s_cs.dma(SC.dma_start(out=out[t * 128:(t + 1) * 128, :], in_=z[:]))
                V.wait_ge(s_cs.sem, st)

            G.wait_ge(s_cs.sem, s_cs.n)
            barrier()

    return nc


def _host_inputs(inputs):
    x = np.asarray(inputs["L0_attn_hidden_states"], dtype=np.float32)
    Wqkv = np.asarray(inputs["L0_attn_Wqkv"], dtype=np.float32)
    Wout = np.asarray(inputs["L0_attn_Wout"], dtype=np.float32)

    shared = {
        "wq": np.ascontiguousarray(Wqkv[0:D, :].T),
        "wkv": np.ascontiguousarray(Wqkv[D:3 * D, :].T),
        "wout": np.ascontiguousarray(Wout.T),
        "wup": np.ascontiguousarray(np.asarray(inputs["L0_ffn_up_w"], dtype=np.float32)),
        "wdown": np.ascontiguousarray(np.asarray(inputs["L0_ffn_down_w"], dtype=np.float32)),
        "wfc1": np.ascontiguousarray(np.asarray(inputs["L0_ffn_fc1_w"], dtype=np.float32)),
        "wfc2": np.ascontiguousarray(np.asarray(inputs["L0_ffn_fc2_w"], dtype=np.float32)),
        "wexp": np.ascontiguousarray(np.asarray(inputs["L0_ffn_expert_w"], dtype=np.float32)),
        "wgate": np.ascontiguousarray(np.asarray(inputs["L0_ffn_gate_w"], dtype=np.float32)),
        "qlnw": np.ascontiguousarray(np.asarray(inputs["L0_attn_q_ln_w"], dtype=np.float32).reshape(1, D)),
        "qlnb": np.ascontiguousarray(np.asarray(inputs["L0_attn_q_ln_b"], dtype=np.float32).reshape(1, D)),
        "klnw": np.ascontiguousarray(np.asarray(inputs["L0_attn_k_ln_w"], dtype=np.float32).reshape(1, D)),
        "klnb": np.ascontiguousarray(np.asarray(inputs["L0_attn_k_ln_b"], dtype=np.float32).reshape(1, D)),
        "gamma": np.ascontiguousarray(np.asarray(inputs["L0_ffn_ln_gamma"], dtype=np.float32).reshape(1, D)),
        "beta": np.ascontiguousarray(np.asarray(inputs["L0_ffn_ln_beta"], dtype=np.float32).reshape(1, D)),
        "tri": np.where(np.arange(128)[None, :] <= np.arange(128)[:, None], 0.0, NEG).astype(np.float32),
        "trit": np.ascontiguousarray(np.where(np.arange(128)[None, :] <= np.arange(128)[:, None], 0.0, NEG).astype(np.float32).T),
        "ident": np.eye(128, dtype=np.float32),
    }

    slopes = _alibi_slopes()
    in_maps = []
    for c in range(8):
        b, half = c // 2, c % 2
        own = x[b, half * HALF:(half + 1) * HALF, :]
        peer = x[b, (1 - half) * HALF:(2 - half) * HALF, :]
        x_seq_c = np.ascontiguousarray(np.concatenate([own, peer], axis=0))
        kg = np.concatenate([
            (1 - half) * HALF + np.arange(HALF),
            half * HALF + np.arange(HALF),
        ]).astype(np.float64)
        kps_c = (slopes[:, None] * kg[None, :]).astype(np.float32)
        if half == 0:
            kps_c[:, 0:HALF] = NEG
        kps_c = np.ascontiguousarray(kps_c)
        maska_c = np.full((1, HALF), NEG if half == 0 else 0.0, dtype=np.float32)
        im = dict(shared)
        im["x_seq"] = x_seq_c
        im["kps"] = kps_c
        im["maska"] = maska_c
        in_maps.append(im)
    return in_maps


_nc_cache = {}


def kernel(**inputs):
    if "full" not in _nc_cache:
        _nc_cache["full"] = build_nc()
    nc = _nc_cache["full"]
    in_maps = _host_inputs(inputs)
    res = run_bass_kernel_spmd(nc, in_maps, core_ids=list(range(8)))
    outp = np.empty((B, S, D), dtype=np.float32)
    for c in range(8):
        b, half = c // 2, c % 2
        outp[b, half * HALF:(half + 1) * HALF, :] = res.results[c]["out"]
    return outp


# version 04
# speedup vs baseline: 1.0842x; 1.0842x over previous
from contextlib import ExitStack

import numpy as np
import concourse.bass as bass
import concourse.mybir as mybir
from concourse.bass_utils import run_bass_kernel_spmd

F32 = mybir.dt.float32
BF16 = mybir.dt.bfloat16
AF = mybir.ActivationFunctionType
ALU = mybir.AluOpType
AX = mybir.AxisListType

B, S, D = 4, 2048, 4096
NH, HD = 32, 128
HALF = 1024
NT_OWN = HALF // 128
NT_SEQ = S // 128
ND = D // 128
NL = 2048 // 128
NI = 8192 // 128
INT = 8192
LAT = 2048
NE = 8
EPS = 1e-5
CLIP = 8.0
SCALE = HD ** -0.5
NEG = -1.0e30


def _alibi_slopes():
    m = np.arange(1, NH + 1, dtype=np.float64) * (8.0 / NH)
    return (1.0 / np.power(2.0, m)).astype(np.float64)


class C:

    def __init__(self, sem):
        self.sem = sem
        self.n = 0

    def inc(self, inst, kk=1):
        self.n += kk
        inst.then_inc(self.sem, kk)
        return self.n

    def dma(self, inst):
        inst.then_inc(self.sem, 16)
        self.n += 16
        return self.n


def build_nc(dbg=False, phases=99, p2part=99, dbg_hj=None):
    nc = bass.Bass("TRN2", target_bir_lowering=False)
    scratch_kind = "ExternalOutput" if dbg else "Internal"

    x_seq = nc.dram_tensor("x_seq", [S, D], F32, kind="ExternalInput").ap()
    wq = nc.dram_tensor("wq", [D, D], F32, kind="ExternalInput").ap()
    wkv = nc.dram_tensor("wkv", [D, 2 * D], F32, kind="ExternalInput").ap()
    wout = nc.dram_tensor("wout", [D, D], F32, kind="ExternalInput").ap()
    wup = nc.dram_tensor("wup", [D, INT], F32, kind="ExternalInput").ap()
    wdown = nc.dram_tensor("wdown", [INT, D], F32, kind="ExternalInput").ap()
    wfc1 = nc.dram_tensor("wfc1", [D, LAT], F32, kind="ExternalInput").ap()
    wfc2 = nc.dram_tensor("wfc2", [LAT, D], F32, kind="ExternalInput").ap()
    wexp = nc.dram_tensor("wexp", [LAT, LAT], F32, kind="ExternalInput").ap()
    wgate = nc.dram_tensor("wgate", [D, NE], F32, kind="ExternalInput").ap()
    qlnw = nc.dram_tensor("qlnw", [1, D], F32, kind="ExternalInput").ap()
    qlnb = nc.dram_tensor("qlnb", [1, D], F32, kind="ExternalInput").ap()
    klnw = nc.dram_tensor("klnw", [1, D], F32, kind="ExternalInput").ap()
    klnb = nc.dram_tensor("klnb", [1, D], F32, kind="ExternalInput").ap()
    gamma = nc.dram_tensor("gamma", [1, D], F32, kind="ExternalInput").ap()
    beta = nc.dram_tensor("beta", [1, D], F32, kind="ExternalInput").ap()
    kps = nc.dram_tensor("kps", [NH, S], F32, kind="ExternalInput").ap()
    maska = nc.dram_tensor("maska", [1, HALF], F32, kind="ExternalInput").ap()
    tri = nc.dram_tensor("tri", [128, 128], F32, kind="ExternalInput").ap()
    trit = nc.dram_tensor("trit", [128, 128], F32, kind="ExternalInput").ap()
    ident = nc.dram_tensor("ident", [128, 128], F32, kind="ExternalInput").ap()

    out = nc.dram_tensor("out", [HALF, D], F32, kind="ExternalOutput").ap()

    q_s = nc.dram_tensor("q_s", [HALF, D], BF16, kind=scratch_kind).ap()
    k_s = nc.dram_tensor("k_s", [S, D], BF16, kind=scratch_kind).ap()
    v_s = nc.dram_tensor("v_s", [NH, NT_SEQ, 128, 128], BF16, kind=scratch_kind).ap()
    qt_s = nc.dram_tensor("qt_s", [NH, 128, HALF], BF16, kind=scratch_kind).ap()
    kt_s = nc.dram_tensor("kt_s", [NH, 128, S], BF16, kind=scratch_kind).ap()
    proj_s = nc.dram_tensor("proj_s", [HALF, D], F32, kind=scratch_kind).ap()
    xp_s = nc.dram_tensor("xp_s", [HALF, D], F32, kind=scratch_kind).ap()
    eo_s = nc.dram_tensor("eo_s", [NL, 128, HALF], BF16, kind=scratch_kind).ap()
    h_s = nc.dram_tensor("h_s", [NI, 128, HALF], BF16, kind=scratch_kind).ap()
    sh_s = nc.dram_tensor("sh_s", [HALF, D], F32, kind=scratch_kind).ap()
    dbg_ao = None
    dbg_x1t = None
    dbg_p2 = None
    if dbg:
        dbg_ao = nc.dram_tensor("dbg_ao", [HALF, D], BF16, kind="ExternalOutput").ap()
        dbg_x1t = nc.dram_tensor("dbg_x1t", [128, ND * S], BF16, kind="ExternalOutput").ap()
        dbg_p2 = nc.dram_tensor("dbg_p2", [128, S + 16], F32, kind="ExternalOutput").ap()


    perm_rows = [HALF + 128 * t if t < NT_OWN else 128 * (t - NT_OWN) for t in range(NT_SEQ)]

    with ExitStack() as top:
        s_gl = C(top.enter_context(nc.semaphore("s_gl")))
        s_sl = C(top.enter_context(nc.semaphore("s_sl")))
        s_cs = C(top.enter_context(nc.semaphore("s_cs")))
        s_t = C(top.enter_context(nc.semaphore("s_t")))
        s_v = C(top.enter_context(nc.semaphore("s_v")))
        s_s = C(top.enter_context(nc.semaphore("s_s")))
        ctrs = [s_gl, s_sl, s_cs, s_t, s_v, s_s]

        T, V, SC, G, SY = nc.tensor, nc.vector, nc.scalar, nc.gpsimd, nc.sync

        def barrier():
            for eng in (T, V, SC, G, SY):
                for c in ctrs:
                    if c.n:
                        eng.wait_ge(c.sem, c.n)

        idsb = top.enter_context(nc.sbuf_tensor("idsb", [128, 128], BF16))
        tri_sb = top.enter_context(nc.sbuf_tensor("tri_sb", [128, 128], F32))
        trit_b = top.enter_context(nc.sbuf_tensor("trit_b", [128, 128], BF16))
        maska_sb = top.enter_context(nc.sbuf_tensor("maska_sb", [1, HALF], BF16))
        ones1b = top.enter_context(nc.sbuf_tensor("ones1b", [1, 128], BF16))
        ones1f = top.enter_context(nc.sbuf_tensor("ones1f", [1, 128], F32))
        oness = top.enter_context(nc.sbuf_tensor("oness", [1, 128], F32))
        s2_all = top.enter_context(nc.sbuf_tensor("s2_all", [128, NT_OWN], F32))

        live_stacks = []

        def early_return():
            for es in reversed(live_stacks):
                es.close()
            return nc

        s_gl.dma(G.dma_start(out=idsb[:], in_=ident))
        s_sl.dma(SY.dma_start(out=tri_sb[:], in_=tri))
        s_gl.dma(G.dma_start(out=maska_sb[:], in_=maska))
        s_gl.dma(G.dma_start(out=trit_b[:], in_=trit))
        G.memset(ones1b[:], 1.0)
        G.memset(ones1f[:], 1.0)
        ii = G.memset(oness[:], SCALE)
        s_gl.inc(ii, 1)
        barrier()

        def emit_stats(src_ap, ws, sc_guard=0):
            stt, agg, t1, t2, rr, t3, nmr = ws
            V.wait_ge(s_v.sem, s_v.n)
            if sc_guard:
                V.wait_ge(s_s.sem, sc_guard)
            for cch in range(8):
                ii = V.bn_stats(stt[:, 6 * cch:6 * cch + 6], src_ap[:, cch * 512:(cch + 1) * 512])
                s_v.inc(ii)
            V.wait_ge(s_v.sem, s_v.n)
            ii = V.bn_aggr(agg[:], stt[:])
            s_v.inc(ii)
            V.wait_ge(s_v.sem, s_v.n)
            ii = V.tensor_scalar_add(t1[:], agg[:, 1:2], EPS)
            s_v.inc(ii)
            V.wait_ge(s_v.sem, s_v.n)
            ii = V.reciprocal(t2[:], t1[:])
            m = s_v.inc(ii)
            SC.wait_ge(s_v.sem, m)
            ii = SC.activation(rr[:], t2[:], AF.Sqrt)
            m2 = s_s.inc(ii)
            V.wait_ge(s_s.sem, m2)
            ii = V.tensor_mul(t3[:], agg[:, 0:1], rr[:])
            s_v.inc(ii)
            V.wait_ge(s_v.sem, s_v.n)
            ii = V.tensor_scalar_mul(nmr[:], t3[:], -1.0)
            return s_v.inc(ii)

        def stats_ws(ctx, pfx):
            return (
                ctx.enter_context(nc.sbuf_tensor(pfx + "stt", [128, 48], F32)),
                ctx.enter_context(nc.sbuf_tensor(pfx + "agg", [128, 2], F32)),
                ctx.enter_context(nc.sbuf_tensor(pfx + "t1", [128, 1], F32)),
                ctx.enter_context(nc.sbuf_tensor(pfx + "t2", [128, 1], F32)),
                ctx.enter_context(nc.sbuf_tensor(pfx + "rr", [128, 1], F32)),
                ctx.enter_context(nc.sbuf_tensor(pfx + "t3", [128, 1], F32)),
                ctx.enter_context(nc.sbuf_tensor(pfx + "nmr", [128, 1], F32)),
            )

        def emit_aggr(stt_ap, ws, sc_guard=0):
            _, agg, t1, t2, rr, t3, nmr = ws
            V.wait_ge(s_v.sem, s_v.n)
            if sc_guard:
                V.wait_ge(s_s.sem, sc_guard)
            ii = V.bn_aggr(agg[:], stt_ap)
            s_v.inc(ii)
            V.wait_ge(s_v.sem, s_v.n)
            ii = V.tensor_scalar_add(t1[:], agg[:, 1:2], EPS)
            s_v.inc(ii)
            V.wait_ge(s_v.sem, s_v.n)
            ii = V.reciprocal(t2[:], t1[:])
            m = s_v.inc(ii)
            SC.wait_ge(s_v.sem, m)
            ii = SC.activation(rr[:], t2[:], AF.Sqrt)
            m2 = s_s.inc(ii)
            V.wait_ge(s_s.sem, m2)
            ii = V.tensor_mul(t3[:], agg[:, 0:1], rr[:])
            s_v.inc(ii)
            V.wait_ge(s_v.sem, s_v.n)
            ii = V.tensor_scalar_mul(nmr[:], t3[:], -1.0)
            return s_v.inc(ii)

        es1 = ExitStack()
        live_stacks.append(es1)
        x1t = es1.enter_context(nc.sbuf_tensor("x1t", [128, ND, S], BF16))

        with ExitStack() as ctx:
            xa = [ctx.enter_context(nc.sbuf_tensor(f"p1xa{i}", [128, D], F32)) for i in range(2)]
            xn = [ctx.enter_context(nc.sbuf_tensor(f"p1xn{i}", [128, D], BF16)) for i in range(2)]
            ws = stats_ws(ctx, "p1")
            rr, nmr = ws[4], ws[6]
            pst = ctx.enter_context(nc.psum_tensor("p1pst", [128, ND * 128], BF16))

            norm_done = []
            tp_t = []
            tp_ev = []
            for t in range(NT_SEQ):
                buf = t % 2
                if t >= 2:
                    SY.wait_ge(s_s.sem, norm_done[t - 2])
                ii = SY.dma_start(out=xa[buf][:], in_=x_seq[t * 128:(t + 1) * 128, :])
                ld = s_sl.dma(ii)
                V.wait_ge(s_sl.sem, ld)
                m = emit_stats(xa[buf], ws)
                SC.wait_ge(s_v.sem, m)
                if t >= 2:
                    SC.wait_ge(s_t.sem, tp_t[t - 2])
                ii = SC.activation(xn[buf][:], xa[buf][:], AF.Identity, bias=nmr[:], scale=rr[:])
                norm_done.append(s_s.inc(ii))
                T.wait_ge(s_s.sem, norm_done[t])
                if t >= 1:
                    T.wait_ge(s_v.sem, tp_ev[t - 1])
                for dch in range(ND):
                    ii = T.transpose(pst[:, dch * 128:(dch + 1) * 128],
                                     xn[buf][:, dch * 128:(dch + 1) * 128], idsb[:])
                tp_t.append(s_t.inc(ii))
                V.wait_ge(s_t.sem, tp_t[t])
                ii = V.tensor_copy(x1t[:, :, t * 128:(t + 1) * 128],
                                   pst[:].rearrange("p (a c) -> p a c", c=128))
                tp_ev.append(s_v.inc(ii))
            if dbg:
                SC.wait_ge(s_v.sem, s_v.n)
                s_cs.dma(SC.dma_start(out=dbg_x1t, in_=x1t[:].rearrange("p a c -> p (a c)")))
            barrier()

        if phases < 2:
            return early_return()

        esS = ExitStack()
        live_stacks.append(esS)
        stt_q = [esS.enter_context(nc.sbuf_tensor(f"sttq{t}", [128, 48], F32, side="right"))
                 for t in range(NT_OWN)]
        stt_k = [esS.enter_context(nc.sbuf_tensor(f"sttk{t}", [128, 48], F32, side="right"))
                 for t in range(NT_SEQ)]
        with ExitStack() as ctx:
            wcol = [ctx.enter_context(nc.sbuf_tensor(f"p1w{i}", [128, ND, 512], BF16)) for i in range(2)]
            ev = [ctx.enter_context(nc.sbuf_tensor(f"p1ev{i}", [128, 512], BF16)) for i in range(4)]
            psA = ctx.enter_context(nc.psum_tensor("p1psA", [128, 2048], F32))

            wq_r = wq.rearrange("(c p) o -> p c o", p=128)
            wkv_r = wkv.rearrange("(c p) o -> p c o", p=128)

            sweeps = []
            for oc in range(D // 512):
                sweeps.append(("q", oc, NT_OWN))
            for oc in range(2 * D // 512):
                sweeps.append(("kv", oc, NT_SEQ))

            base_t = s_t.n
            fin_mark = []
            acc = base_t
            for (_, _, ntl) in sweeps:
                acc += ntl
                fin_mark.append(acc)

            wld = []
            for si, (kind, oc, ntl) in enumerate(sweeps):
                buf = si % 2
                if si >= 2:
                    G.wait_ge(s_t.sem, fin_mark[si - 2])
                src = wq_r[:, :, oc * 512:(oc + 1) * 512] if kind == "q" else \
                    wkv_r[:, :, oc * 512:(oc + 1) * 512]
                wld.append(s_gl.dma(G.dma_start(out=wcol[buf][:], in_=src)))

            ev_done = {}
            st_done = {}
            it = 0
            for si, (kind, oc, ntl) in enumerate(sweeps):
                buf = si % 2
                T.wait_ge(s_gl.sem, wld[si])
                for t in range(ntl):
                    bank = it % 4
                    if bank in ev_done:
                        T.wait_ge(s_v.sem, ev_done[bank])
                    for d in range(ND):
                        ii = T.matmul(psA[:, bank * 512:(bank + 1) * 512],
                                      x1t[:, d, t * 128:(t + 1) * 128],
                                      wcol[buf][:, d, :],
                                      start=(d == 0), stop=(d == ND - 1))
                    mm = s_t.inc(ii)
                    V.wait_ge(s_t.sem, mm)
                    V.wait_ge(s_cs.sem, st_done.get(bank, 0))
                    ii = V.tensor_scalar(ev[bank][:], psA[:, bank * 512:(bank + 1) * 512],
                                         -CLIP, CLIP, ALU.max, ALU.min)
                    clip_m = s_v.inc(ii)
                    if kind == "q" or oc < D // 512:
                        stt_dst = stt_q[t] if kind == "q" else stt_k[perm_rows[t] // 128]
                        V.wait_ge(s_v.sem, clip_m)
                        ii = V.bn_stats(stt_dst[:, oc * 6:(oc + 1) * 6], ev[bank][:])
                        ev_done[bank] = s_v.inc(ii)
                    else:
                        ev_done[bank] = clip_m
                    SC.wait_ge(s_v.sem, clip_m)
                    if kind == "q":
                        dst = q_s[t * 128:(t + 1) * 128, oc * 512:(oc + 1) * 512]
                    elif oc < D // 512:
                        r0 = perm_rows[t]
                        dst = k_s[r0:r0 + 128, oc * 512:(oc + 1) * 512]
                    else:
                        c0 = perm_rows[t] // 128
                        h0 = (oc - D // 512) * 4
                        dst = v_s[h0:h0 + 4, c0, :, :].transpose([1, 0, 2])
                        st_done[bank] = s_cs.dma(SC.dma_start(
                            out=dst, in_=ev[bank][:].rearrange("p (a c) -> p a c", c=128)))
                        it += 1
                        continue
                    st_done[bank] = s_cs.dma(SC.dma_start(out=dst, in_=ev[bank][:]))
                    it += 1
            barrier()
        es1.close()
        live_stacks.remove(es1)

        if phases < 3:
            return early_return()

        with ExitStack() as ctx:
            wbt = [ctx.enter_context(nc.sbuf_tensor(f"p2w{i}", [128, D], F32)) for i in range(2)]
            vec1 = ctx.enter_context(nc.sbuf_tensor("p2vec", [1, D], F32))
            qk = [ctx.enter_context(nc.sbuf_tensor(f"p2qk{i}", [128, D], BF16)) for i in range(2)]
            qn = [ctx.enter_context(nc.sbuf_tensor(f"p2qn{i}", [128, D], F32)) for i in range(2)]
            qh = [ctx.enter_context(nc.sbuf_tensor(f"p2qh{i}", [128, D], BF16)) for i in range(2)]
            qe = [ctx.enter_context(nc.sbuf_tensor(f"p2qe{i}", [128, D], BF16)) for i in range(2)]
            ws2 = [stats_ws(ctx, f"p2s{i}") for i in range(2)]
            psW = ctx.enter_context(nc.psum_tensor("p2psW", [128, 2048], F32))
            pst = ctx.enter_context(nc.psum_tensor("p2pst", [128, ND * 128], BF16))

            p2_gi = 0
            mul_m = [0] * 24
            tp_m = [0] * 24
            ev_m = [0] * 24
            st_m = [0] * 24
            agg_m = [0] * 24
            nor_m = [0] * 24
            for which in ("q", "k"):
                wv, bv = (qlnw, qlnb) if which == "q" else (klnw, klnb)
                lhs1 = oness if which == "q" else ones1f
                for wi, vecap in enumerate((wv, bv)):
                    SY.wait_ge(s_v.sem, s_v.n)
                    ld = s_sl.dma(SY.dma_start(out=vec1[:], in_=vecap))
                    T.wait_ge(s_sl.sem, ld)
                    T.wait_ge(s_v.sem, s_v.n)
                    for hhf in range(2):
                        for cch in range(4):
                            ii = T.matmul(psW[:, cch * 512:(cch + 1) * 512], lhs1[:],
                                          vec1[:, hhf * 2048 + cch * 512:hhf * 2048 + (cch + 1) * 512],
                                          start=True, stop=True)
                        m = s_t.inc(ii)
                        V.wait_ge(s_t.sem, m)
                        V.wait_ge(s_v.sem, s_v.n)
                        ii = V.tensor_copy(wbt[wi][:, hhf * 2048:(hhf + 1) * 2048], psW[:])
                        s_v.inc(ii)
                        T.wait_ge(s_v.sem, s_v.n)

                ntile = NT_OWN if which == "q" else NT_SEQ
                for t in range(ntile):
                    gi = p2_gi
                    buf = gi % 2
                    if gi >= 2:
                        SY.wait_ge(s_s.sem, nor_m[gi - 2])
                    src_ap = q_s[t * 128:(t + 1) * 128, :] if which == "q"                         else k_s[t * 128:(t + 1) * 128, :]
                    ld = s_sl.dma(SY.dma_start(out=qk[buf][:], in_=src_ap))
                    stt_ap = stt_q[t][:] if which == "q" else stt_k[t][:]
                    agg_m[gi] = emit_aggr(stt_ap, ws2[buf],
                                          sc_guard=nor_m[gi - 2] if gi >= 2 else 0)
                    SC.wait_ge(s_v.sem, agg_m[gi])
                    SC.wait_ge(s_sl.sem, ld)
                    if gi >= 2:
                        SC.wait_ge(s_v.sem, mul_m[gi - 2])
                    ii = SC.activation(qn[buf][:], qk[buf][:], AF.Identity,
                                       bias=ws2[buf][6][:], scale=ws2[buf][4][:])
                    nor_m[gi] = s_s.inc(ii)
                    V.wait_ge(s_s.sem, nor_m[gi])
                    if gi >= 2:
                        V.wait_ge(s_t.sem, tp_m[gi - 2])
                    ii = V.tensor_mul(qn[buf][:], qn[buf][:], wbt[0][:])
                    s_v.inc(ii)
                    V.wait_ge(s_v.sem, s_v.n)
                    ii = V.tensor_add(qh[buf][:], qn[buf][:], wbt[1][:])
                    mul_m[gi] = s_v.inc(ii)
                    T.wait_ge(s_v.sem, mul_m[gi])
                    if gi >= 1:
                        T.wait_ge(s_s.sem, ev_m[gi - 1])
                    for dch in range(ND):
                        ii = T.transpose(pst[:, dch * 128:(dch + 1) * 128],
                                         qh[buf][:, dch * 128:(dch + 1) * 128], idsb[:])
                    tp_m[gi] = s_t.inc(ii)
                    SC.wait_ge(s_t.sem, tp_m[gi])
                    if gi >= 2:
                        SC.wait_ge(s_cs.sem, st_m[gi - 2])
                    ii = SC.copy(qe[buf][:], pst[:])
                    ev_m[gi] = s_s.inc(ii)
                    SC.wait_ge(s_s.sem, ev_m[gi])
                    for hh in range(NH):
                        if which == "q":
                            dst = qt_s[hh, :, t * 128:(t + 1) * 128]
                        else:
                            dst = kt_s[hh, :, t * 128:(t + 1) * 128]
                        stm = s_cs.dma(
                            SC.dma_start(out=dst, in_=qe[buf][:, hh * 128:(hh + 1) * 128]))
                    st_m[gi] = stm
                    p2_gi += 1
            barrier()
        esS.close()
        live_stacks.remove(esS)

        if phases < 4:
            return early_return()

        es2 = ExitStack()
        live_stacks.append(es2)
        ao = es2.enter_context(nc.sbuf_tensor("ao", [128, NT_OWN, D], BF16))
        with ExitStack() as ctx:
            kt = [ctx.enter_context(nc.sbuf_tensor(f"p3kt{i}", [128, S], BF16)) for i in range(2)]
            vh = [ctx.enter_context(nc.sbuf_tensor(f"p3vh{i}", [128, NT_SEQ, 128], BF16)) for i in range(2)]
            qt = [ctx.enter_context(nc.sbuf_tensor(f"p3qt{i}", [128, HALF], BF16)) for i in range(2)]
            kr = [ctx.enter_context(nc.sbuf_tensor(f"p3kr{i}", [1, S], F32)) for i in range(2)]
            bias2 = [ctx.enter_context(nc.sbuf_tensor(f"p3bias{i}", [128, S], F32)) for i in range(2)]
            ssb2 = [ctx.enter_context(nc.sbuf_tensor(f"p3ssb{i}", [128, S], F32)) for i in range(2)]
            pb2 = [ctx.enter_context(nc.sbuf_tensor(f"p3pb{i}", [128, S], BF16)) for i in range(2)]
            ptb = ctx.enter_context(nc.sbuf_tensor("p3ptb", [128, S], BF16))
            negm2 = [ctx.enter_context(nc.sbuf_tensor(f"p3negm{i}", [128, 1], F32)) for i in range(2)]
            lsum2 = [ctx.enter_context(nc.sbuf_tensor(f"p3lsum{i}", [128, 1], F32)) for i in range(2)]
            rl2 = [ctx.enter_context(nc.sbuf_tensor(f"p3rl{i}", [128, 1], F32)) for i in range(2)]
            psS = ctx.enter_context(nc.psum_tensor("p3psS", [128, 2048], F32))
            psT = ctx.enter_context(nc.psum_tensor("p3psT", [128, 2048], BF16))
            psV = ctx.enter_context(nc.psum_tensor("p3psV", [128, 256], F32))
            psB1 = ctx.enter_context(nc.psum_tensor("p3psB1", [128, 512], F32))


            def load_head(hh, guard_mark):
                buf = hh % 2
                if guard_mark:
                    SY.wait_ge(s_t.sem, guard_mark)
                s_sl.dma(SY.dma_start(out=kt[buf][:], in_=kt_s[hh]))
                s_sl.dma(SY.dma_start(out=qt[buf][:], in_=qt_s[hh]))
                s_sl.dma(SY.dma_start(out=vh[buf][:], in_=v_s[hh].transpose([1, 0, 2])))
                return s_sl.dma(SY.dma_start(out=kr[buf][:], in_=kps[hh:hh + 1, :]))

            rl4 = [ctx.enter_context(nc.sbuf_tensor(f"p3rl4_{i}", [128, 1], F32)) for i in range(4)]

            iters = [(hh, j) for hh in range(NH) for j in range(NT_OWN)]
            NIT = len(iters)
            sc_mark = [0] * NIT
            tt_mark = [0] * NIT
            mx_mark = [0] * NIT
            ex_mark = [0] * NIT
            tpm = [0] * NIT
            cp_mark = [0] * NIT
            pv_mark = [0] * NIT
            ev_mark = [0] * NIT
            rl_mark = [0] * NIT
            last_pv = [0] * NH
            bias_built = [0] * (NH + 1)
            bias_ev = [0]
            head_ld = {0: load_head(0, 0)}

            def LJ(i):
                return HALF + 128 * (iters[i][1] + 1)

            def build_bias_chunk(hh_next, cch, guard_tt):
                nbuf = hh_next % 2
                T.wait_ge(s_sl.sem, head_ld[hh_next])
                if bias_ev[0]:
                    T.wait_ge(s_v.sem, bias_ev[0])
                ii = T.matmul(psB1[:], ones1f[:],
                              kr[nbuf][:, cch * 512:(cch + 1) * 512], start=True, stop=True)
                m = s_t.inc(ii)
                V.wait_ge(s_t.sem, m)
                if guard_tt:
                    V.wait_ge(s_v.sem, guard_tt)
                ii = V.tensor_copy(bias2[nbuf][:, cch * 512:(cch + 1) * 512], psB1[:])
                bias_ev[0] = s_v.inc(ii)
                return bias_ev[0]

            for cch in range(4):
                bias_built[0] = build_bias_chunk(0, cch, 0)

            for i in range(NIT + 3):
                if i < NIT:
                    hh, j = iters[i]
                    L = LJ(i)
                    buf = hh % 2
                    if j == 0:
                        T.wait_ge(s_v.sem, bias_built[hh])
                        T.wait_ge(s_sl.sem, head_ld[hh])
                    if i >= 1:
                        T.wait_ge(s_v.sem, tt_mark[i - 1])
                    lastbank = ((L - 1) // 512) * 512
                    for c0 in range(0, L, 512):
                        w = min(512, L - c0)
                        ii = T.matmul(psS[:, c0:c0 + w],
                                      qt[buf][:, j * 128:(j + 1) * 128],
                                      kt[buf][:, c0:c0 + w],
                                      start=True, stop=(c0 != lastbank))
                    ii = T.matmul(psS[:, L - 128:L], trit_b[:], idsb[:],
                                  start=False, stop=True)
                    sc_mark[i] = s_t.inc(ii)
                    if j == 3 and hh + 1 < NH:
                        head_ld[hh + 1] = load_head(hh + 1, last_pv[hh - 1] if hh >= 1 else 0)
                    if 4 <= j <= 7 and hh + 1 < NH:
                        bias_built[hh + 1] = build_bias_chunk(
                            hh + 1, j - 4, tt_mark[i] if j == 4 else 0)
                if i >= 2 and i - 2 < NIT:
                    k = i - 2
                    nck2 = LJ(k) // 128
                    T.wait_ge(s_s.sem, ex_mark[k])
                    if k >= 1:
                        T.wait_ge(s_s.sem, cp_mark[k - 1])
                    for c in range(nck2):
                        ii = T.transpose(psT[:, c * 128:(c + 1) * 128],
                                         pb2[k % 2][:, c * 128:(c + 1) * 128], idsb[:])
                    tpm[k] = s_t.inc(ii)
                if i >= 3 and i - 3 < NIT:
                    k = i - 3
                    hh3, j3 = iters[k]
                    nck3 = LJ(k) // 128
                    slot = k % 2
                    T.wait_ge(s_s.sem, cp_mark[k])
                    if k >= 2:
                        T.wait_ge(s_s.sem, ev_mark[k - 2])
                    for c in range(nck3):
                        ii = T.matmul(psV[:, slot * 128:(slot + 1) * 128],
                                      ptb[:, c * 128:(c + 1) * 128],
                                      vh[hh3 % 2][:, c, :],
                                      start=(c == 0), stop=(c == nck3 - 1))
                    pv_mark[k] = s_t.inc(ii)
                    if j3 == NT_OWN - 1:
                        last_pv[hh3] = pv_mark[k]
                if i >= 1 and i - 1 < NIT:
                    k = i - 1
                    Lk = LJ(k)
                    SC.wait_ge(s_v.sem, mx_mark[k])
                    if k >= 2:
                        SC.wait_ge(s_t.sem, tpm[k - 2])
                    ii = SC.activation(pb2[k % 2][:, 0:Lk], ssb2[k % 2][:, 0:Lk], AF.Exp,
                                       bias=negm2[k % 2][:], scale=1.0,
                                       accum_out=lsum2[k % 2][:])
                    ex_mark[k] = s_s.inc(ii)
                if i >= 2 and i - 2 < NIT:
                    k = i - 2
                    nckk = LJ(k) // 128
                    SC.wait_ge(s_t.sem, tpm[k])
                    if k >= 1:
                        SC.wait_ge(s_t.sem, pv_mark[k - 1])
                    ii = SC.copy(ptb[:, 0:nckk * 128], psT[:, 0:nckk * 128])
                    cp_mark[k] = s_s.inc(ii)
                if i >= 3 and i - 3 < NIT:
                    k = i - 3
                    hk, jk = iters[k]
                    slot = k % 2
                    SC.wait_ge(s_t.sem, pv_mark[k])
                    SC.wait_ge(s_v.sem, rl_mark[k])
                    ii = SC.activation(ao[:, jk, hk * 128:(hk + 1) * 128],
                                       psV[:, slot * 128:(slot + 1) * 128],
                                       AF.Identity, bias=0.0, scale=rl4[k % 4][:])
                    ev_mark[k] = s_s.inc(ii)
                if i < NIT:
                    hh, j = iters[i]
                    L = LJ(i)
                    V.wait_ge(s_t.sem, sc_mark[i])
                    if i >= 2:
                        V.wait_ge(s_s.sem, ex_mark[i - 2])
                    ii = V.tensor_add(ssb2[i % 2][:, 0:L], psS[:, 0:L], bias2[hh % 2][:, 0:L])
                    tt_mark[i] = s_v.inc(ii)
                    V.wait_ge(s_v.sem, tt_mark[i])
                    ii = V.reduce_max(negm2[i % 2][:], ssb2[i % 2][:, 0:L],
                                      axis=AX.X, negate=True)
                    mx_mark[i] = s_v.inc(ii)
                if i >= 1 and i - 1 < NIT:
                    k = i - 1
                    V.wait_ge(s_s.sem, ex_mark[k])
                    ii = V.reciprocal(rl4[k % 4][:], lsum2[k % 2][:])
                    rl_mark[k] = s_v.inc(ii)
            if dbg:
                SC.wait_ge(s_v.sem, s_v.n)
                SC.wait_ge(s_s.sem, s_s.n)
                SC.wait_ge(s_t.sem, s_t.n)
                G.wait_ge(s_s.sem, s_s.n)
                G.wait_ge(s_v.sem, s_v.n)
                s_gl.dma(G.dma_start(out=dbg_p2[:, 0:S], in_=pb2[(NIT - 1) % 2][:]))
                s_cs.dma(SC.dma_start(out=dbg_p2[:, S:S + 1], in_=lsum2[(NIT - 1) % 2][:], allow_slow_non_contiguous=True))
                s_cs.dma(SC.dma_start(out=dbg_p2[:, S + 1:S + 2], in_=rl4[(NIT - 1) % 4][:], allow_slow_non_contiguous=True))
                s_cs.dma(SC.dma_start(out=dbg_p2[:, S + 2:S + 3], in_=negm2[(NIT - 1) % 2][:], allow_slow_non_contiguous=True))
                for j in range(NT_OWN):
                    s_cs.dma(SC.dma_start(out=dbg_ao[j * 128:(j + 1) * 128, :], in_=ao[:, j, :]))
            barrier()

        if phases < 5:
            return early_return()

        es3 = ExitStack()
        live_stacks.append(es3)
        aot = es3.enter_context(nc.sbuf_tensor("aot", [128, ND, HALF], BF16, side="right"))
        with ExitStack() as ctx:
            pst = ctx.enter_context(nc.psum_tensor("p4pst", [128, ND * 128], BF16))
            for j in range(NT_OWN):
                if j >= 1:
                    T.wait_ge(s_v.sem, s_v.n)
                for dch in range(ND):
                    ii = T.transpose(pst[:, dch * 128:(dch + 1) * 128],
                                     ao[:, j, dch * 128:(dch + 1) * 128], idsb[:])
                m = s_t.inc(ii)
                V.wait_ge(s_t.sem, m)
                ii = V.tensor_copy(aot[:, :, j * 128:(j + 1) * 128],
                                   pst[:].rearrange("p (a c) -> p a c", c=128))
                s_v.inc(ii)
            barrier()
        es2.close()
        live_stacks.remove(es2)

        with ExitStack() as ctx:
            wcol = [ctx.enter_context(nc.sbuf_tensor(f"p4w{i}", [128, ND, 512], BF16)) for i in range(2)]
            ev = [ctx.enter_context(nc.sbuf_tensor(f"p4ev{i}", [128, 512], F32)) for i in range(4)]
            psA = ctx.enter_context(nc.psum_tensor("p4psA", [128, 2048], F32))
            wout_r = wout.rearrange("(c p) o -> p c o", p=128)
            nsw = D // 512
            base_t = s_t.n
            fin_mark = [base_t + (oc + 1) * NT_OWN for oc in range(nsw)]
            wld = []
            for oc in range(nsw):
                buf = oc % 2
                if oc >= 2:
                    G.wait_ge(s_t.sem, fin_mark[oc - 2])
                wld.append(s_gl.dma(G.dma_start(out=wcol[buf][:],
                                                in_=wout_r[:, :, oc * 512:(oc + 1) * 512])))
            ev_done = {}
            st_done = {}
            for oc in range(nsw):
                buf = oc % 2
                T.wait_ge(s_gl.sem, wld[oc])
                for t in range(NT_OWN):
                    bank = (oc * NT_OWN + t) % 4
                    if bank in ev_done:
                        T.wait_ge(s_v.sem, ev_done[bank])
                    for d in range(ND):
                        ii = T.matmul(psA[:, bank * 512:(bank + 1) * 512],
                                      aot[:, d, t * 128:(t + 1) * 128],
                                      wcol[buf][:, d, :],
                                      start=(d == 0), stop=(d == ND - 1))
                    mm = s_t.inc(ii)
                    V.wait_ge(s_t.sem, mm)
                    V.wait_ge(s_cs.sem, st_done.get(bank, 0))
                    ii = V.tensor_copy(ev[bank][:], psA[:, bank * 512:(bank + 1) * 512])
                    ev_done[bank] = s_v.inc(ii)
                    SC.wait_ge(s_v.sem, ev_done[bank])
                    st_done[bank] = s_cs.dma(
                        SC.dma_start(out=proj_s[t * 128:(t + 1) * 128, oc * 512:(oc + 1) * 512],
                                     in_=ev[bank][:]))
            barrier()
        es3.close()
        live_stacks.remove(es3)

        if phases < 6:
            return early_return()

        es4 = ExitStack()
        live_stacks.append(es4)
        x2t = es4.enter_context(nc.sbuf_tensor("x2t", [128, ND, HALF], BF16))
        with ExitStack() as ctx:
            pr = [ctx.enter_context(nc.sbuf_tensor(f"p5pr{i}", [128, D], F32)) for i in range(2)]
            xt_ = [ctx.enter_context(nc.sbuf_tensor(f"p5xt{i}", [128, D], F32)) for i in range(2)]
            x2b = [ctx.enter_context(nc.sbuf_tensor(f"p5x2{i}", [128, D], BF16)) for i in range(2)]
            ws_a = stats_ws(ctx, "p5a")
            ws_b = stats_ws(ctx, "p5b")
            pst = ctx.enter_context(nc.psum_tensor("p5pst", [128, ND * 128], BF16))

            tp_t = [0] * NT_OWN
            ev_t = [0] * NT_OWN
            st_mark = [0] * NT_OWN
            na_m = [0] * NT_OWN
            nb_m = [0] * NT_OWN
            for t in range(NT_OWN):
                buf = t % 2
                if t >= 2:
                    SY.wait_ge(s_t.sem, tp_t[t - 2])
                    SY.wait_ge(s_cs.sem, st_mark[t - 2])
                ld = s_sl.dma(SY.dma_start(out=pr[buf][:], in_=proj_s[t * 128:(t + 1) * 128, :]))
                ld = s_sl.dma(SY.dma_start(out=xt_[buf][:], in_=x_seq[t * 128:(t + 1) * 128, :]))
                V.wait_ge(s_sl.sem, ld)
                m = emit_stats(pr[buf], ws_a, sc_guard=na_m[t - 1] if t >= 1 else 0)
                SC.wait_ge(s_v.sem, m)
                ii = SC.activation(pr[buf][:], pr[buf][:], AF.Identity,
                                   bias=ws_a[6][:], scale=ws_a[4][:])
                na_m[t] = s_s.inc(ii)
                V.wait_ge(s_s.sem, na_m[t])
                ii = V.tensor_add(xt_[buf][:], xt_[buf][:], pr[buf][:])
                xp_m = s_v.inc(ii)
                SC.wait_ge(s_v.sem, xp_m)
                st_mark[t] = s_cs.dma(SC.dma_start(out=xp_s[t * 128:(t + 1) * 128, :], in_=xt_[buf][:]))
                m = emit_stats(xt_[buf], ws_b, sc_guard=nb_m[t - 1] if t >= 1 else 0)
                SC.wait_ge(s_v.sem, m)
                ii = SC.activation(x2b[buf][:], xt_[buf][:], AF.Identity,
                                   bias=ws_b[6][:], scale=ws_b[4][:])
                nb_m[t] = s_s.inc(ii)
                T.wait_ge(s_s.sem, nb_m[t])
                if t >= 1:
                    T.wait_ge(s_v.sem, ev_t[t - 1])
                for dch in range(ND):
                    ii = T.transpose(pst[:, dch * 128:(dch + 1) * 128],
                                     x2b[buf][:, dch * 128:(dch + 1) * 128], idsb[:])
                tp_t[t] = s_t.inc(ii)
                V.wait_ge(s_t.sem, tp_t[t])
                ii = V.tensor_copy(x2t[:, :, t * 128:(t + 1) * 128],
                                   pst[:].rearrange("p (a c) -> p a c", c=128))
                ev_t[t] = s_v.inc(ii)
            barrier()

        if phases < 7:
            return early_return()

        with ExitStack() as ctx:
            gw = ctx.enter_context(nc.sbuf_tensor("p6gw", [128, ND, NE], BF16))
            lg = ctx.enter_context(nc.sbuf_tensor("p6lg", [128, NE], F32))
            pw = ctx.enter_context(nc.sbuf_tensor("p6pw", [128, 7, 7], F32))
            mk = ctx.enter_context(nc.sbuf_tensor("p6mk", [128, NE - 1], F32))
            psA = ctx.enter_context(nc.psum_tensor("p6psA", [128, 512], F32))
            s_gl.dma(G.dma_start(out=gw[:], in_=wgate.rearrange("(c p) o -> p c o", p=128)))
            T.wait_ge(s_gl.sem, s_gl.n)
            for t in range(NT_OWN):
                if t >= 1:
                    T.wait_ge(s_v.sem, s_v.n)
                for d in range(ND):
                    ii = T.matmul(psA[:, 0:NE], x2t[:, d, t * 128:(t + 1) * 128], gw[:, d, :],
                                  start=(d == 0), stop=(d == ND - 1))
                m = s_t.inc(ii)
                V.wait_ge(s_t.sem, m)
                ii = V.tensor_copy(lg[:], psA[:, 0:NE])
                s_v.inc(ii)
                V.wait_ge(s_v.sem, s_v.n)
                for kk in range(1, NE):
                    ii = V.tensor_add(pw[:, kk - 1, 0:NE - kk], lg[:, 0:NE - kk], lg[:, kk:NE])
                    s_v.inc(ii)
                V.wait_ge(s_v.sem, s_v.n)
                for kk in range(1, NE):
                    ii = V.reduce_max(mk[:, kk - 1:kk], pw[:, kk - 1, 0:NE - kk], axis=AX.X)
                    s_v.inc(ii)
                V.wait_ge(s_v.sem, s_v.n)
                ii = V.reduce_max(s2_all[:, t:t + 1], mk[:], axis=AX.X)
                s_v.inc(ii)
            barrier()

        es5 = ExitStack()
        live_stacks.append(es5)
        xlat = es5.enter_context(nc.sbuf_tensor("xlat", [128, NL, HALF], BF16, side="right"))
        with ExitStack() as ctx:
            wt_ = [ctx.enter_context(nc.sbuf_tensor(f"p7w{i}", [128, ND, 128], BF16)) for i in range(2)]
            psA = ctx.enter_context(nc.psum_tensor("p7psA", [128, 2048], F32))
            wfc1_r = wfc1.rearrange("(c p) o -> p c o", p=128)
            base_t = s_t.n
            fin_mark = [base_t + (lc + 1) * 2 for lc in range(NL)]
            wld = []
            for lc in range(NL):
                buf = lc % 2
                if lc >= 2:
                    G.wait_ge(s_t.sem, fin_mark[lc - 2])
                wld.append(s_gl.dma(G.dma_start(out=wt_[buf][:],
                                                in_=wfc1_r[:, :, lc * 128:(lc + 1) * 128])))
            evd = {}
            for lc in range(NL):
                buf = lc % 2
                T.wait_ge(s_gl.sem, wld[lc])
                for tb in range(2):
                    bank = (lc * 2 + tb) % 4
                    if bank in evd:
                        T.wait_ge(s_v.sem, evd[bank])
                    for d in range(ND):
                        ii = T.matmul(psA[:, bank * 512:(bank + 1) * 512],
                                      wt_[buf][:, d, :],
                                      x2t[:, d, tb * 512:(tb + 1) * 512],
                                      start=(d == 0), stop=(d == ND - 1))
                    mm = s_t.inc(ii)
                    V.wait_ge(s_t.sem, mm)
                    ii = V.tensor_copy(xlat[:, lc, tb * 512:(tb + 1) * 512],
                                       psA[:, bank * 512:(bank + 1) * 512])
                    evd[bank] = s_v.inc(ii)
            barrier()

        es6 = ExitStack()
        live_stacks.append(es6)
        eo = es6.enter_context(nc.sbuf_tensor("eo", [128, NL, HALF], BF16, side="right"))
        with ExitStack() as ctx:
            wt_ = [ctx.enter_context(nc.sbuf_tensor(f"p8w{i}", [128, NL, 128], BF16)) for i in range(2)]
            psA = ctx.enter_context(nc.psum_tensor("p8psA", [128, 2048], F32))
            wexp_r = wexp.rearrange("(c p) o -> p c o", p=128)
            base_t = s_t.n
            fin_mark = [base_t + (mc + 1) * 2 for mc in range(NL)]
            wld = []
            for mc in range(NL):
                buf = mc % 2
                if mc >= 2:
                    G.wait_ge(s_t.sem, fin_mark[mc - 2])
                wld.append(s_gl.dma(G.dma_start(out=wt_[buf][:],
                                                in_=wexp_r[:, :, mc * 128:(mc + 1) * 128])))
            evd = {}
            for mc in range(NL):
                buf = mc % 2
                T.wait_ge(s_gl.sem, wld[mc])
                for tb in range(2):
                    bank = (mc * 2 + tb) % 4
                    if bank in evd:
                        T.wait_ge(s_v.sem, evd[bank])
                    for lc in range(NL):
                        ii = T.matmul(psA[:, bank * 512:(bank + 1) * 512],
                                      wt_[buf][:, lc, :],
                                      xlat[:, lc, tb * 512:(tb + 1) * 512],
                                      start=(lc == 0), stop=(lc == NL - 1))
                    mm = s_t.inc(ii)
                    V.wait_ge(s_t.sem, mm)
                    ii = V.tensor_copy(eo[:, mc, tb * 512:(tb + 1) * 512],
                                       psA[:, bank * 512:(bank + 1) * 512])
                    evd[bank] = s_v.inc(ii)
            SC.wait_ge(s_v.sem, s_v.n)
            s_cs.dma(SC.dma_start(out=eo_s.rearrange("a p c -> p a c"), in_=eo[:]))
            barrier()
        es6.close()
        live_stacks.remove(es6)
        es5.close()
        live_stacks.remove(es5)

        if phases < 8:
            return early_return()

        with ExitStack() as ctx:
            wt_ = [ctx.enter_context(nc.sbuf_tensor(f"p9w{i}", [128, ND, 128], BF16)) for i in range(2)]
            htmp = [ctx.enter_context(nc.sbuf_tensor(f"p9h{i}", [128, 512], F32)) for i in range(2)]
            hb = [ctx.enter_context(nc.sbuf_tensor(f"p9hb{i}", [128, 512], BF16)) for i in range(4)]
            psA = ctx.enter_context(nc.psum_tensor("p9psA", [128, 2048], F32))
            wup_r = wup.rearrange("(c p) o -> p c o", p=128)
            base_t = s_t.n
            fin_mark = [base_t + (ic + 1) * 2 for ic in range(NI)]
            wld = []
            for ic in range(NI):
                buf = ic % 2
                if ic >= 2:
                    G.wait_ge(s_t.sem, fin_mark[ic - 2])
                wld.append(s_gl.dma(G.dma_start(out=wt_[buf][:],
                                                in_=wup_r[:, :, ic * 128:(ic + 1) * 128])))
            evd = {}
            st_done = {}
            for ic in range(NI):
                buf = ic % 2
                T.wait_ge(s_gl.sem, wld[ic])
                for tb in range(2):
                    it2 = ic * 2 + tb
                    bank = it2 % 4
                    if bank in evd:
                        T.wait_ge(s_s.sem, evd[bank])
                    for d in range(ND):
                        ii = T.matmul(psA[:, bank * 512:(bank + 1) * 512],
                                      wt_[buf][:, d, :],
                                      x2t[:, d, tb * 512:(tb + 1) * 512],
                                      start=(d == 0), stop=(d == ND - 1))
                    mm = s_t.inc(ii)
                    ebuf = it2 % 4
                    SC.wait_ge(s_t.sem, mm)
                    SC.wait_ge(s_cs.sem, st_done.get(ebuf, 0))
                    ii = SC.activation(htmp[it2 % 2][:], psA[:, bank * 512:(bank + 1) * 512], AF.Relu)
                    evd[bank] = s_s.inc(ii)
                    SC.wait_ge(s_s.sem, evd[bank])
                    ii = SC.activation(hb[ebuf][:], htmp[it2 % 2][:], AF.Square)
                    m = s_s.inc(ii)
                    SC.wait_ge(s_s.sem, m)
                    st_done[ebuf] = s_cs.dma(
                        SC.dma_start(out=h_s[ic, :, tb * 512:(tb + 1) * 512], in_=hb[ebuf][:]))
            barrier()
        es4.close()
        live_stacks.remove(es4)

        if phases < 9:
            return early_return()

        with ExitStack() as ctx:
            hh_ = ctx.enter_context(nc.sbuf_tensor("p10hh", [128, NI, 512], BF16))
            wcol = [ctx.enter_context(nc.sbuf_tensor(f"p10w{i}", [128, NI, 512], BF16)) for i in range(2)]
            ev = [ctx.enter_context(nc.sbuf_tensor(f"p10ev{i}", [128, 512], F32)) for i in range(4)]
            psA = ctx.enter_context(nc.psum_tensor("p10psA", [128, 2048], F32))
            wdown_r = wdown.rearrange("(c p) o -> p c o", p=128)
            h_r = h_s.rearrange("a p c -> p a c")
            for th in range(2):
                SY.wait_ge(s_t.sem, s_t.n)
                h_ld = s_sl.dma(SY.dma_start(out=hh_[:], in_=h_r[:, :, th * 512:(th + 1) * 512]))
                base_t = s_t.n
                fin_mark = [base_t + (oc + 1) * 4 for oc in range(8)]
                wld = []
                for oc in range(8):
                    buf = oc % 2
                    if oc >= 2:
                        G.wait_ge(s_t.sem, fin_mark[oc - 2])
                    wld.append(s_gl.dma(G.dma_start(out=wcol[buf][:],
                                                    in_=wdown_r[:, :, oc * 512:(oc + 1) * 512])))
                evd = {}
                st_done = {}
                for oc in range(8):
                    buf = oc % 2
                    T.wait_ge(s_gl.sem, wld[oc])
                    T.wait_ge(s_sl.sem, h_ld)
                    for t in range(4):
                        bank = (oc * 4 + t) % 4
                        if bank in evd:
                            T.wait_ge(s_v.sem, evd[bank])
                        for i in range(NI):
                            ii = T.matmul(psA[:, bank * 512:(bank + 1) * 512],
                                          hh_[:, i, t * 128:(t + 1) * 128],
                                          wcol[buf][:, i, :],
                                          start=(i == 0), stop=(i == NI - 1))
                        mm = s_t.inc(ii)
                        V.wait_ge(s_t.sem, mm)
                        V.wait_ge(s_cs.sem, st_done.get(bank, 0))
                        ii = V.tensor_copy(ev[bank][:], psA[:, bank * 512:(bank + 1) * 512])
                        evd[bank] = s_v.inc(ii)
                        SC.wait_ge(s_v.sem, evd[bank])
                        r0 = th * 512 + t * 128
                        st_done[bank] = s_cs.dma(
                            SC.dma_start(out=sh_s[r0:r0 + 128, oc * 512:(oc + 1) * 512],
                                         in_=ev[bank][:]))
            barrier()

        if phases < 10:
            return early_return()

        with ExitStack() as ctx:
            wf2 = ctx.enter_context(nc.sbuf_tensor("p11w", [128, NL, D], BF16))
            eot = [ctx.enter_context(nc.sbuf_tensor(f"p11e{i}", [128, NL, 128], BF16)) for i in range(2)]
            shx = ctx.enter_context(nc.sbuf_tensor("p11sx", [128, D], F32))
            z = ctx.enter_context(nc.sbuf_tensor("p11z", [128, D], F32))
            gb = ctx.enter_context(nc.sbuf_tensor("p11gb", [128, D], F32))
            bb = ctx.enter_context(nc.sbuf_tensor("p11bb", [128, D], F32))
            vec1 = ctx.enter_context(nc.sbuf_tensor("p11vec", [1, 512], F32))
            ws_a = stats_ws(ctx, "p11a")
            ws_b = stats_ws(ctx, "p11b")
            psA = ctx.enter_context(nc.psum_tensor("p11psA", [128, 2048], F32))
            psB = ctx.enter_context(nc.psum_tensor("p11psB", [128, 2048], F32))

            s_gl.dma(G.dma_start(out=wf2[:], in_=wfc2.rearrange("(c p) o -> p c o", p=128)))

            for wi, vecap in enumerate((gamma, beta)):
                dstt = gb if wi == 0 else bb
                for cch in range(8):
                    T.wait_ge(s_v.sem, s_v.n)
                    SY.wait_ge(s_t.sem, s_t.n)
                    ld = s_sl.dma(SY.dma_start(out=vec1[:], in_=vecap[0:1, cch * 512:(cch + 1) * 512]))
                    T.wait_ge(s_sl.sem, ld)
                    ii = T.matmul(psA[:, 0:512], ones1f[:], vec1[:], start=True, stop=True)
                    m = s_t.inc(ii)
                    V.wait_ge(s_t.sem, m)
                    ii = V.tensor_copy(dstt[:, cch * 512:(cch + 1) * 512], psA[:, 0:512])
                    s_v.inc(ii)

            T.wait_ge(s_gl.sem, s_gl.n)
            zev = 0
            mm_t = [0] * NT_OWN
            fin_t = [0] * NT_OWN
            na_m = [0] * NT_OWN
            nb_m = [0] * NT_OWN
            eld = [0] * NT_OWN
            for t in range(NT_OWN):
                buf = t % 2
                if t >= 2:
                    SY.wait_ge(s_t.sem, mm_t[t - 2])
                eld[t] = s_sl.dma(SY.dma_start(out=eot[buf][:],
                                      in_=eo_s.rearrange("a p c -> p a c")[:, :, t * 128:(t + 1) * 128]))
                if t >= 1:
                    SY.wait_ge(s_v.sem, fin_t[t - 1])
                ld = s_sl.dma(SY.dma_start(out=shx[:], in_=sh_s[t * 128:(t + 1) * 128, :]))
                T.wait_ge(s_sl.sem, eld[t])
                if zev:
                    T.wait_ge(s_v.sem, zev)
                for mc in range(NL):
                    for o in range(8):
                        pbank = psA if o < 4 else psB
                        ii = T.matmul(pbank[:, (o % 4) * 512:((o % 4) + 1) * 512],
                                      eot[buf][:, mc, :],
                                      wf2[:, mc, o * 512:(o + 1) * 512],
                                      start=(mc == 0), stop=(mc == NL - 1))
                mm = s_t.inc(ii)
                mm_t[t] = mm
                V.wait_ge(s_t.sem, mm)
                V.wait_ge(s_sl.sem, ld)
                for o in range(8):
                    pbank = psA if o < 4 else psB
                    ii = V.tensor_scalar(z[:, o * 512:(o + 1) * 512],
                                         pbank[:, (o % 4) * 512:((o % 4) + 1) * 512],
                                         s2_all[:, t:t + 1], None, ALU.mult)
                    zev = s_v.inc(ii)
                V.wait_ge(s_v.sem, zev)
                ii = V.tensor_add(z[:], z[:], shx[:])
                m_shc = s_v.inc(ii)
                SY.wait_ge(s_v.sem, m_shc)
                ld2 = s_sl.dma(SY.dma_start(out=shx[:], in_=xp_s[t * 128:(t + 1) * 128, :]))
                m = emit_stats(z, ws_a, sc_guard=na_m[t - 1] if t >= 1 else 0)
                SC.wait_ge(s_v.sem, m)
                ii = SC.activation(z[:], z[:], AF.Identity, bias=ws_a[6][:], scale=ws_a[4][:])
                na_m[t] = s_s.inc(ii)
                V.wait_ge(s_s.sem, na_m[t])
                ii = V.tensor_mul(z[:], z[:], gb[:])
                s_v.inc(ii)
                V.wait_ge(s_v.sem, s_v.n)
                ii = V.tensor_add(z[:], z[:], bb[:])
                s_v.inc(ii)
                m = emit_stats(z, ws_b, sc_guard=nb_m[t - 1] if t >= 1 else 0)
                SC.wait_ge(s_v.sem, m)
                ii = SC.activation(z[:], z[:], AF.Identity, bias=ws_b[6][:], scale=ws_b[4][:])
                nb_m[t] = s_s.inc(ii)
                V.wait_ge(s_s.sem, nb_m[t])
                V.wait_ge(s_sl.sem, ld2)
                ii = V.tensor_add(z[:], z[:], shx[:])
                fin = s_v.inc(ii)
                fin_t[t] = fin
                SC.wait_ge(s_v.sem, fin)
                st = s_cs.dma(SC.dma_start(out=out[t * 128:(t + 1) * 128, :], in_=z[:]))
                V.wait_ge(s_cs.sem, st)

            G.wait_ge(s_cs.sem, s_cs.n)
            barrier()

    return nc


def _host_inputs(inputs):
    x = np.asarray(inputs["L0_attn_hidden_states"], dtype=np.float32)
    Wqkv = np.asarray(inputs["L0_attn_Wqkv"], dtype=np.float32)
    Wout = np.asarray(inputs["L0_attn_Wout"], dtype=np.float32)

    shared = {
        "wq": np.ascontiguousarray(Wqkv[0:D, :].T),
        "wkv": np.ascontiguousarray(Wqkv[D:3 * D, :].T),
        "wout": np.ascontiguousarray(Wout.T),
        "wup": np.ascontiguousarray(np.asarray(inputs["L0_ffn_up_w"], dtype=np.float32)),
        "wdown": np.ascontiguousarray(np.asarray(inputs["L0_ffn_down_w"], dtype=np.float32)),
        "wfc1": np.ascontiguousarray(np.asarray(inputs["L0_ffn_fc1_w"], dtype=np.float32)),
        "wfc2": np.ascontiguousarray(np.asarray(inputs["L0_ffn_fc2_w"], dtype=np.float32)),
        "wexp": np.ascontiguousarray(np.asarray(inputs["L0_ffn_expert_w"], dtype=np.float32)),
        "wgate": np.ascontiguousarray(np.asarray(inputs["L0_ffn_gate_w"], dtype=np.float32)),
        "qlnw": np.ascontiguousarray(np.asarray(inputs["L0_attn_q_ln_w"], dtype=np.float32).reshape(1, D)),
        "qlnb": np.ascontiguousarray(np.asarray(inputs["L0_attn_q_ln_b"], dtype=np.float32).reshape(1, D)),
        "klnw": np.ascontiguousarray(np.asarray(inputs["L0_attn_k_ln_w"], dtype=np.float32).reshape(1, D)),
        "klnb": np.ascontiguousarray(np.asarray(inputs["L0_attn_k_ln_b"], dtype=np.float32).reshape(1, D)),
        "gamma": np.ascontiguousarray(np.asarray(inputs["L0_ffn_ln_gamma"], dtype=np.float32).reshape(1, D)),
        "beta": np.ascontiguousarray(np.asarray(inputs["L0_ffn_ln_beta"], dtype=np.float32).reshape(1, D)),
        "tri": np.where(np.arange(128)[None, :] <= np.arange(128)[:, None], 0.0, NEG).astype(np.float32),
        "trit": np.ascontiguousarray(np.where(np.arange(128)[None, :] <= np.arange(128)[:, None], 0.0, NEG).astype(np.float32).T),
        "ident": np.eye(128, dtype=np.float32),
    }

    slopes = _alibi_slopes()
    in_maps = []
    for c in range(8):
        b, half = c // 2, c % 2
        own = x[b, half * HALF:(half + 1) * HALF, :]
        peer = x[b, (1 - half) * HALF:(2 - half) * HALF, :]
        x_seq_c = np.ascontiguousarray(np.concatenate([own, peer], axis=0))
        kg = np.concatenate([
            (1 - half) * HALF + np.arange(HALF),
            half * HALF + np.arange(HALF),
        ]).astype(np.float64)
        kps_c = (slopes[:, None] * kg[None, :]).astype(np.float32)
        if half == 0:
            kps_c[:, 0:HALF] = NEG
        kps_c = np.ascontiguousarray(kps_c)
        maska_c = np.full((1, HALF), NEG if half == 0 else 0.0, dtype=np.float32)
        im = dict(shared)
        im["x_seq"] = x_seq_c
        im["kps"] = kps_c
        im["maska"] = maska_c
        in_maps.append(im)
    return in_maps


_nc_cache = {}


def kernel(**inputs):
    if "full" not in _nc_cache:
        _nc_cache["full"] = build_nc()
    nc = _nc_cache["full"]
    in_maps = _host_inputs(inputs)
    res = run_bass_kernel_spmd(nc, in_maps, core_ids=list(range(8)))
    outp = np.empty((B, S, D), dtype=np.float32)
    for c in range(8):
        b, half = c // 2, c % 2
        outp[b, half * HALF:(half + 1) * HALF, :] = res.results[c]["out"]
    return outp


# version 05
# speedup vs baseline: 1.0982x; 1.0129x over previous
from contextlib import ExitStack

import numpy as np
import concourse.bass as bass
import concourse.mybir as mybir
from concourse.bass_utils import run_bass_kernel_spmd

F32 = mybir.dt.float32
BF16 = mybir.dt.bfloat16
AF = mybir.ActivationFunctionType
ALU = mybir.AluOpType
AX = mybir.AxisListType

B, S, D = 4, 2048, 4096
NH, HD = 32, 128
HALF = 1024
NT_OWN = HALF // 128
NT_SEQ = S // 128
ND = D // 128
NL = 2048 // 128
NI = 8192 // 128
INT = 8192
LAT = 2048
NE = 8
EPS = 1e-5
CLIP = 8.0
SCALE = HD ** -0.5
NEG = -1.0e30


def _alibi_slopes():
    m = np.arange(1, NH + 1, dtype=np.float64) * (8.0 / NH)
    return (1.0 / np.power(2.0, m)).astype(np.float64)


class C:

    def __init__(self, sem):
        self.sem = sem
        self.n = 0

    def inc(self, inst, kk=1):
        self.n += kk
        inst.then_inc(self.sem, kk)
        return self.n

    def dma(self, inst):
        inst.then_inc(self.sem, 16)
        self.n += 16
        return self.n


def build_nc(dbg=False, phases=99, p2part=99, dbg_hj=None):
    nc = bass.Bass("TRN2", target_bir_lowering=False)
    scratch_kind = "ExternalOutput" if dbg else "Internal"

    x_seq = nc.dram_tensor("x_seq", [S, D], F32, kind="ExternalInput").ap()
    wq = nc.dram_tensor("wq", [D, D], F32, kind="ExternalInput").ap()
    wkv = nc.dram_tensor("wkv", [D, 2 * D], F32, kind="ExternalInput").ap()
    wout = nc.dram_tensor("wout", [D, D], F32, kind="ExternalInput").ap()
    wup = nc.dram_tensor("wup", [D, INT], F32, kind="ExternalInput").ap()
    wdown = nc.dram_tensor("wdown", [INT, D], F32, kind="ExternalInput").ap()
    wfc1 = nc.dram_tensor("wfc1", [D, LAT], F32, kind="ExternalInput").ap()
    wfc2 = nc.dram_tensor("wfc2", [LAT, D], F32, kind="ExternalInput").ap()
    wexp = nc.dram_tensor("wexp", [LAT, LAT], F32, kind="ExternalInput").ap()
    wgate = nc.dram_tensor("wgate", [D, NE], F32, kind="ExternalInput").ap()
    qlnw = nc.dram_tensor("qlnw", [1, D], F32, kind="ExternalInput").ap()
    qlnb = nc.dram_tensor("qlnb", [1, D], F32, kind="ExternalInput").ap()
    klnw = nc.dram_tensor("klnw", [1, D], F32, kind="ExternalInput").ap()
    klnb = nc.dram_tensor("klnb", [1, D], F32, kind="ExternalInput").ap()
    gamma = nc.dram_tensor("gamma", [1, D], F32, kind="ExternalInput").ap()
    beta = nc.dram_tensor("beta", [1, D], F32, kind="ExternalInput").ap()
    kps = nc.dram_tensor("kps", [NH, S], F32, kind="ExternalInput").ap()
    maska = nc.dram_tensor("maska", [1, HALF], F32, kind="ExternalInput").ap()
    tri = nc.dram_tensor("tri", [128, 128], F32, kind="ExternalInput").ap()
    trit = nc.dram_tensor("trit", [128, 128], F32, kind="ExternalInput").ap()
    ident = nc.dram_tensor("ident", [128, 128], F32, kind="ExternalInput").ap()

    out = nc.dram_tensor("out", [HALF, D], F32, kind="ExternalOutput").ap()

    q_s = nc.dram_tensor("q_s", [HALF, D], BF16, kind=scratch_kind).ap()
    k_s = nc.dram_tensor("k_s", [S, D], BF16, kind=scratch_kind).ap()
    v_s = nc.dram_tensor("v_s", [NH, NT_SEQ, 128, 128], BF16, kind=scratch_kind).ap()
    qt_s = nc.dram_tensor("qt_s", [NH, 128, HALF], BF16, kind=scratch_kind).ap()
    kt_s = nc.dram_tensor("kt_s", [NH, 128, S], BF16, kind=scratch_kind).ap()
    proj_s = nc.dram_tensor("proj_s", [HALF, D], F32, kind=scratch_kind).ap()
    xp_s = nc.dram_tensor("xp_s", [HALF, D], F32, kind=scratch_kind).ap()
    eo_s = nc.dram_tensor("eo_s", [NL, 128, HALF], BF16, kind=scratch_kind).ap()
    h_s = nc.dram_tensor("h_s", [NI, 128, HALF], BF16, kind=scratch_kind).ap()
    sh_s = nc.dram_tensor("sh_s", [HALF, D], F32, kind=scratch_kind).ap()
    dbg_ao = None
    dbg_x1t = None
    dbg_p2 = None
    if dbg:
        dbg_ao = nc.dram_tensor("dbg_ao", [HALF, D], BF16, kind="ExternalOutput").ap()
        dbg_x1t = nc.dram_tensor("dbg_x1t", [128, ND * S], BF16, kind="ExternalOutput").ap()
        dbg_p2 = nc.dram_tensor("dbg_p2", [128, S + 16], F32, kind="ExternalOutput").ap()


    perm_rows = [HALF + 128 * t if t < NT_OWN else 128 * (t - NT_OWN) for t in range(NT_SEQ)]

    with ExitStack() as top:
        s_gl = C(top.enter_context(nc.semaphore("s_gl")))
        s_sl = C(top.enter_context(nc.semaphore("s_sl")))
        s_cs = C(top.enter_context(nc.semaphore("s_cs")))
        s_t = C(top.enter_context(nc.semaphore("s_t")))
        s_v = C(top.enter_context(nc.semaphore("s_v")))
        s_s = C(top.enter_context(nc.semaphore("s_s")))
        ctrs = [s_gl, s_sl, s_cs, s_t, s_v, s_s]

        T, V, SC, G, SY = nc.tensor, nc.vector, nc.scalar, nc.gpsimd, nc.sync

        def barrier():
            for eng in (T, V, SC, G, SY):
                for c in ctrs:
                    if c.n:
                        eng.wait_ge(c.sem, c.n)

        idsb = top.enter_context(nc.sbuf_tensor("idsb", [128, 128], BF16))
        tri_sb = top.enter_context(nc.sbuf_tensor("tri_sb", [128, 128], F32))
        trit_b = top.enter_context(nc.sbuf_tensor("trit_b", [128, 128], BF16))
        maska_sb = top.enter_context(nc.sbuf_tensor("maska_sb", [1, HALF], BF16))
        ones1b = top.enter_context(nc.sbuf_tensor("ones1b", [1, 128], BF16))
        ones1f = top.enter_context(nc.sbuf_tensor("ones1f", [1, 128], F32))
        oness = top.enter_context(nc.sbuf_tensor("oness", [1, 128], F32))
        s2_all = top.enter_context(nc.sbuf_tensor("s2_all", [128, NT_OWN], F32))

        live_stacks = []

        def early_return():
            for es in reversed(live_stacks):
                es.close()
            return nc

        s_gl.dma(G.dma_start(out=idsb[:], in_=ident))
        s_sl.dma(SY.dma_start(out=tri_sb[:], in_=tri))
        s_gl.dma(G.dma_start(out=maska_sb[:], in_=maska))
        s_gl.dma(G.dma_start(out=trit_b[:], in_=trit))
        G.memset(ones1b[:], 1.0)
        G.memset(ones1f[:], 1.0)
        ii = G.memset(oness[:], SCALE)
        s_gl.inc(ii, 1)
        barrier()

        def emit_stats(src_ap, ws, sc_guard=0):
            stt, agg, t1, t2, rr, t3, nmr = ws
            V.wait_ge(s_v.sem, s_v.n)
            if sc_guard:
                V.wait_ge(s_s.sem, sc_guard)
            for cch in range(8):
                ii = V.bn_stats(stt[:, 6 * cch:6 * cch + 6], src_ap[:, cch * 512:(cch + 1) * 512])
                s_v.inc(ii)
            V.wait_ge(s_v.sem, s_v.n)
            ii = V.bn_aggr(agg[:], stt[:])
            s_v.inc(ii)
            V.wait_ge(s_v.sem, s_v.n)
            ii = V.tensor_scalar_add(t1[:], agg[:, 1:2], EPS)
            s_v.inc(ii)
            V.wait_ge(s_v.sem, s_v.n)
            ii = V.reciprocal(t2[:], t1[:])
            m = s_v.inc(ii)
            SC.wait_ge(s_v.sem, m)
            ii = SC.activation(rr[:], t2[:], AF.Sqrt)
            m2 = s_s.inc(ii)
            V.wait_ge(s_s.sem, m2)
            ii = V.tensor_mul(t3[:], agg[:, 0:1], rr[:])
            s_v.inc(ii)
            V.wait_ge(s_v.sem, s_v.n)
            ii = V.tensor_scalar_mul(nmr[:], t3[:], -1.0)
            return s_v.inc(ii)

        def stats_ws(ctx, pfx):
            return (
                ctx.enter_context(nc.sbuf_tensor(pfx + "stt", [128, 48], F32)),
                ctx.enter_context(nc.sbuf_tensor(pfx + "agg", [128, 2], F32)),
                ctx.enter_context(nc.sbuf_tensor(pfx + "t1", [128, 1], F32)),
                ctx.enter_context(nc.sbuf_tensor(pfx + "t2", [128, 1], F32)),
                ctx.enter_context(nc.sbuf_tensor(pfx + "rr", [128, 1], F32)),
                ctx.enter_context(nc.sbuf_tensor(pfx + "t3", [128, 1], F32)),
                ctx.enter_context(nc.sbuf_tensor(pfx + "nmr", [128, 1], F32)),
            )

        def emit_aggr(stt_ap, ws, sc_guard=0):
            _, agg, t1, t2, rr, t3, nmr = ws
            V.wait_ge(s_v.sem, s_v.n)
            if sc_guard:
                V.wait_ge(s_s.sem, sc_guard)
            ii = V.bn_aggr(agg[:], stt_ap)
            s_v.inc(ii)
            V.wait_ge(s_v.sem, s_v.n)
            ii = V.tensor_scalar_add(t1[:], agg[:, 1:2], EPS)
            s_v.inc(ii)
            V.wait_ge(s_v.sem, s_v.n)
            ii = V.reciprocal(t2[:], t1[:])
            m = s_v.inc(ii)
            SC.wait_ge(s_v.sem, m)
            ii = SC.activation(rr[:], t2[:], AF.Sqrt)
            m2 = s_s.inc(ii)
            V.wait_ge(s_s.sem, m2)
            ii = V.tensor_mul(t3[:], agg[:, 0:1], rr[:])
            s_v.inc(ii)
            V.wait_ge(s_v.sem, s_v.n)
            ii = V.tensor_scalar_mul(nmr[:], t3[:], -1.0)
            return s_v.inc(ii)

        es1 = ExitStack()
        live_stacks.append(es1)
        x1t = es1.enter_context(nc.sbuf_tensor("x1t", [128, ND, S], BF16))

        with ExitStack() as ctx:
            xa = [ctx.enter_context(nc.sbuf_tensor(f"p1xa{i}", [128, D], F32)) for i in range(2)]
            xn = [ctx.enter_context(nc.sbuf_tensor(f"p1xn{i}", [128, D], BF16)) for i in range(2)]
            ws = stats_ws(ctx, "p1")
            rr, nmr = ws[4], ws[6]
            pst = ctx.enter_context(nc.psum_tensor("p1pst", [128, ND * 128], BF16))

            norm_done = []
            tp_t = []
            tp_ev = []
            for t in range(NT_SEQ):
                buf = t % 2
                if t >= 2:
                    SY.wait_ge(s_s.sem, norm_done[t - 2])
                ii = SY.dma_start(out=xa[buf][:], in_=x_seq[t * 128:(t + 1) * 128, :])
                ld = s_sl.dma(ii)
                V.wait_ge(s_sl.sem, ld)
                m = emit_stats(xa[buf], ws, sc_guard=norm_done[t - 1] if t >= 1 else 0)
                SC.wait_ge(s_v.sem, m)
                if t >= 2:
                    SC.wait_ge(s_t.sem, tp_t[t - 2])
                ii = SC.activation(xn[buf][:], xa[buf][:], AF.Identity, bias=nmr[:], scale=rr[:])
                norm_done.append(s_s.inc(ii))
                T.wait_ge(s_s.sem, norm_done[t])
                if t >= 1:
                    T.wait_ge(s_v.sem, tp_ev[t - 1])
                for dch in range(ND):
                    ii = T.transpose(pst[:, dch * 128:(dch + 1) * 128],
                                     xn[buf][:, dch * 128:(dch + 1) * 128], idsb[:])
                tp_t.append(s_t.inc(ii))
                V.wait_ge(s_t.sem, tp_t[t])
                ii = V.tensor_copy(x1t[:, :, t * 128:(t + 1) * 128],
                                   pst[:].rearrange("p (a c) -> p a c", c=128))
                tp_ev.append(s_v.inc(ii))
            if dbg:
                SC.wait_ge(s_v.sem, s_v.n)
                s_cs.dma(SC.dma_start(out=dbg_x1t, in_=x1t[:].rearrange("p a c -> p (a c)")))
            barrier()

        if phases < 2:
            return early_return()

        esS = ExitStack()
        live_stacks.append(esS)
        stt_q = [esS.enter_context(nc.sbuf_tensor(f"sttq{t}", [128, 48], F32, side="right"))
                 for t in range(NT_OWN)]
        stt_k = [esS.enter_context(nc.sbuf_tensor(f"sttk{t}", [128, 48], F32, side="right"))
                 for t in range(NT_SEQ)]
        with ExitStack() as ctx:
            wcol = [ctx.enter_context(nc.sbuf_tensor(f"p1w{i}", [128, ND, 512], BF16)) for i in range(2)]
            ev = [ctx.enter_context(nc.sbuf_tensor(f"p1ev{i}", [128, 512], BF16)) for i in range(4)]
            psA = ctx.enter_context(nc.psum_tensor("p1psA", [128, 2048], F32))

            wq_r = wq.rearrange("(c p) o -> p c o", p=128)
            wkv_r = wkv.rearrange("(c p) o -> p c o", p=128)

            sweeps = []
            for oc in range(D // 512):
                sweeps.append(("q", oc, NT_OWN))
            for oc in range(2 * D // 512):
                sweeps.append(("kv", oc, NT_SEQ))

            base_t = s_t.n
            fin_mark = []
            acc = base_t
            for (_, _, ntl) in sweeps:
                acc += ntl
                fin_mark.append(acc)

            wld = []
            for si, (kind, oc, ntl) in enumerate(sweeps):
                buf = si % 2
                if si >= 2:
                    G.wait_ge(s_t.sem, fin_mark[si - 2])
                src = wq_r[:, :, oc * 512:(oc + 1) * 512] if kind == "q" else \
                    wkv_r[:, :, oc * 512:(oc + 1) * 512]
                wld.append(s_gl.dma(G.dma_start(out=wcol[buf][:], in_=src)))

            ev_done = {}
            st_done = {}
            it = 0
            for si, (kind, oc, ntl) in enumerate(sweeps):
                buf = si % 2
                T.wait_ge(s_gl.sem, wld[si])
                for t in range(ntl):
                    bank = it % 4
                    if bank in ev_done:
                        T.wait_ge(s_v.sem, ev_done[bank])
                    for d in range(ND):
                        ii = T.matmul(psA[:, bank * 512:(bank + 1) * 512],
                                      x1t[:, d, t * 128:(t + 1) * 128],
                                      wcol[buf][:, d, :],
                                      start=(d == 0), stop=(d == ND - 1))
                    mm = s_t.inc(ii)
                    V.wait_ge(s_t.sem, mm)
                    V.wait_ge(s_cs.sem, st_done.get(bank, 0))
                    ii = V.tensor_scalar(ev[bank][:], psA[:, bank * 512:(bank + 1) * 512],
                                         -CLIP, CLIP, ALU.max, ALU.min)
                    clip_m = s_v.inc(ii)
                    if kind == "q" or oc < D // 512:
                        stt_dst = stt_q[t] if kind == "q" else stt_k[perm_rows[t] // 128]
                        V.wait_ge(s_v.sem, clip_m)
                        ii = V.bn_stats(stt_dst[:, oc * 6:(oc + 1) * 6], ev[bank][:])
                        ev_done[bank] = s_v.inc(ii)
                    else:
                        ev_done[bank] = clip_m
                    SC.wait_ge(s_v.sem, clip_m)
                    if kind == "q":
                        dst = q_s[t * 128:(t + 1) * 128, oc * 512:(oc + 1) * 512]
                    elif oc < D // 512:
                        r0 = perm_rows[t]
                        dst = k_s[r0:r0 + 128, oc * 512:(oc + 1) * 512]
                    else:
                        c0 = perm_rows[t] // 128
                        h0 = (oc - D // 512) * 4
                        dst = v_s[h0:h0 + 4, c0, :, :].transpose([1, 0, 2])
                        st_done[bank] = s_cs.dma(SC.dma_start(
                            out=dst, in_=ev[bank][:].rearrange("p (a c) -> p a c", c=128)))
                        it += 1
                        continue
                    st_done[bank] = s_cs.dma(SC.dma_start(out=dst, in_=ev[bank][:]))
                    it += 1
            barrier()
        es1.close()
        live_stacks.remove(es1)

        if phases < 3:
            return early_return()

        with ExitStack() as ctx:
            wbt = [ctx.enter_context(nc.sbuf_tensor(f"p2w{i}", [128, D], F32)) for i in range(2)]
            vec1 = ctx.enter_context(nc.sbuf_tensor("p2vec", [1, D], F32))
            qk = [ctx.enter_context(nc.sbuf_tensor(f"p2qk{i}", [128, D], BF16)) for i in range(2)]
            qn = [ctx.enter_context(nc.sbuf_tensor(f"p2qn{i}", [128, D], F32)) for i in range(2)]
            qh = [ctx.enter_context(nc.sbuf_tensor(f"p2qh{i}", [128, D], BF16)) for i in range(2)]
            qe = [ctx.enter_context(nc.sbuf_tensor(f"p2qe{i}", [128, D], BF16)) for i in range(2)]
            ws2 = [stats_ws(ctx, f"p2s{i}") for i in range(2)]
            psW = ctx.enter_context(nc.psum_tensor("p2psW", [128, 2048], F32))
            pst = ctx.enter_context(nc.psum_tensor("p2pst", [128, ND * 128], BF16))

            p2_gi = 0
            mul_m = [0] * 24
            tp_m = [0] * 24
            ev_m = [0] * 24
            st_m = [0] * 24
            agg_m = [0] * 24
            nor_m = [0] * 24
            for which in ("q", "k"):
                wv, bv = (qlnw, qlnb) if which == "q" else (klnw, klnb)
                lhs1 = oness if which == "q" else ones1f
                for wi, vecap in enumerate((wv, bv)):
                    SY.wait_ge(s_v.sem, s_v.n)
                    ld = s_sl.dma(SY.dma_start(out=vec1[:], in_=vecap))
                    T.wait_ge(s_sl.sem, ld)
                    T.wait_ge(s_v.sem, s_v.n)
                    for hhf in range(2):
                        for cch in range(4):
                            ii = T.matmul(psW[:, cch * 512:(cch + 1) * 512], lhs1[:],
                                          vec1[:, hhf * 2048 + cch * 512:hhf * 2048 + (cch + 1) * 512],
                                          start=True, stop=True)
                        m = s_t.inc(ii)
                        V.wait_ge(s_t.sem, m)
                        V.wait_ge(s_v.sem, s_v.n)
                        ii = V.tensor_copy(wbt[wi][:, hhf * 2048:(hhf + 1) * 2048], psW[:])
                        s_v.inc(ii)
                        T.wait_ge(s_v.sem, s_v.n)

                ntile = NT_OWN if which == "q" else NT_SEQ
                for t in range(ntile):
                    gi = p2_gi
                    buf = gi % 2
                    if gi >= 2:
                        SY.wait_ge(s_s.sem, nor_m[gi - 2])
                    src_ap = q_s[t * 128:(t + 1) * 128, :] if which == "q"                         else k_s[t * 128:(t + 1) * 128, :]
                    ld = s_sl.dma(SY.dma_start(out=qk[buf][:], in_=src_ap))
                    stt_ap = stt_q[t][:] if which == "q" else stt_k[t][:]
                    agg_m[gi] = emit_aggr(stt_ap, ws2[buf],
                                          sc_guard=nor_m[gi - 2] if gi >= 2 else 0)
                    SC.wait_ge(s_v.sem, agg_m[gi])
                    SC.wait_ge(s_sl.sem, ld)
                    if gi >= 2:
                        SC.wait_ge(s_v.sem, mul_m[gi - 2])
                    ii = SC.activation(qn[buf][:], qk[buf][:], AF.Identity,
                                       bias=ws2[buf][6][:], scale=ws2[buf][4][:])
                    nor_m[gi] = s_s.inc(ii)
                    V.wait_ge(s_s.sem, nor_m[gi])
                    if gi >= 2:
                        V.wait_ge(s_t.sem, tp_m[gi - 2])
                    ii = V.tensor_mul(qn[buf][:], qn[buf][:], wbt[0][:])
                    s_v.inc(ii)
                    V.wait_ge(s_v.sem, s_v.n)
                    ii = V.tensor_add(qh[buf][:], qn[buf][:], wbt[1][:])
                    mul_m[gi] = s_v.inc(ii)
                    T.wait_ge(s_v.sem, mul_m[gi])
                    if gi >= 1:
                        T.wait_ge(s_s.sem, ev_m[gi - 1])
                    for dch in range(ND):
                        ii = T.transpose(pst[:, dch * 128:(dch + 1) * 128],
                                         qh[buf][:, dch * 128:(dch + 1) * 128], idsb[:])
                    tp_m[gi] = s_t.inc(ii)
                    SC.wait_ge(s_t.sem, tp_m[gi])
                    if gi >= 2:
                        SC.wait_ge(s_cs.sem, st_m[gi - 2])
                    ii = SC.copy(qe[buf][:], pst[:])
                    ev_m[gi] = s_s.inc(ii)
                    SC.wait_ge(s_s.sem, ev_m[gi])
                    if which == "q":
                        dst = qt_s[:, :, t * 128:(t + 1) * 128].transpose([1, 0, 2])
                    else:
                        dst = kt_s[:, :, t * 128:(t + 1) * 128].transpose([1, 0, 2])
                    st_m[gi] = s_cs.dma(SC.dma_start(
                        out=dst, in_=qe[buf][:].rearrange("p (a c) -> p a c", c=128)))
                    p2_gi += 1
            barrier()
        esS.close()
        live_stacks.remove(esS)

        if phases < 4:
            return early_return()

        es2 = ExitStack()
        live_stacks.append(es2)
        ao = es2.enter_context(nc.sbuf_tensor("ao", [128, NT_OWN, D], BF16))
        with ExitStack() as ctx:
            kt = [ctx.enter_context(nc.sbuf_tensor(f"p3kt{i}", [128, S], BF16)) for i in range(2)]
            vh = [ctx.enter_context(nc.sbuf_tensor(f"p3vh{i}", [128, NT_SEQ, 128], BF16)) for i in range(2)]
            qt = [ctx.enter_context(nc.sbuf_tensor(f"p3qt{i}", [128, HALF], BF16)) for i in range(2)]
            kr = [ctx.enter_context(nc.sbuf_tensor(f"p3kr{i}", [1, S], F32)) for i in range(2)]
            bias2 = [ctx.enter_context(nc.sbuf_tensor(f"p3bias{i}", [128, S], F32)) for i in range(2)]
            ssb2 = [ctx.enter_context(nc.sbuf_tensor(f"p3ssb{i}", [128, S], F32)) for i in range(2)]
            pb2 = [ctx.enter_context(nc.sbuf_tensor(f"p3pb{i}", [128, S], BF16)) for i in range(2)]
            ptb = ctx.enter_context(nc.sbuf_tensor("p3ptb", [128, S], BF16))
            negm2 = [ctx.enter_context(nc.sbuf_tensor(f"p3negm{i}", [128, 1], F32)) for i in range(2)]
            lsum2 = [ctx.enter_context(nc.sbuf_tensor(f"p3lsum{i}", [128, 1], F32)) for i in range(2)]
            rl2 = [ctx.enter_context(nc.sbuf_tensor(f"p3rl{i}", [128, 1], F32)) for i in range(2)]
            psS = ctx.enter_context(nc.psum_tensor("p3psS", [128, 2048], F32))
            psT = ctx.enter_context(nc.psum_tensor("p3psT", [128, 2048], BF16))
            psV = ctx.enter_context(nc.psum_tensor("p3psV", [128, 256], F32))
            psB1 = ctx.enter_context(nc.psum_tensor("p3psB1", [128, 512], F32))


            def load_head(hh, guard_mark):
                buf = hh % 2
                if guard_mark:
                    SY.wait_ge(s_t.sem, guard_mark)
                s_sl.dma(SY.dma_start(out=kt[buf][:], in_=kt_s[hh]))
                s_sl.dma(SY.dma_start(out=qt[buf][:], in_=qt_s[hh]))
                s_sl.dma(SY.dma_start(out=vh[buf][:], in_=v_s[hh].transpose([1, 0, 2])))
                return s_sl.dma(SY.dma_start(out=kr[buf][:], in_=kps[hh:hh + 1, :]))

            rl4 = [ctx.enter_context(nc.sbuf_tensor(f"p3rl4_{i}", [128, 1], F32)) for i in range(4)]

            iters = [(hh, j) for hh in range(NH) for j in range(NT_OWN)]
            NIT = len(iters)
            sc_mark = [0] * NIT
            tt_mark = [0] * NIT
            mx_mark = [0] * NIT
            ex_mark = [0] * NIT
            tpm = [0] * NIT
            cp_mark = [0] * NIT
            pv_mark = [0] * NIT
            ev_mark = [0] * NIT
            rl_mark = [0] * NIT
            last_pv = [0] * NH
            bias_built = [0] * (NH + 1)
            bias_ev = [0]
            head_ld = {0: load_head(0, 0)}

            def LJ(i):
                return HALF + 128 * (iters[i][1] + 1)

            def build_bias_chunk(hh_next, cch, guard_tt):
                nbuf = hh_next % 2
                T.wait_ge(s_sl.sem, head_ld[hh_next])
                if bias_ev[0]:
                    T.wait_ge(s_v.sem, bias_ev[0])
                ii = T.matmul(psB1[:], ones1f[:],
                              kr[nbuf][:, cch * 512:(cch + 1) * 512], start=True, stop=True)
                m = s_t.inc(ii)
                V.wait_ge(s_t.sem, m)
                if guard_tt:
                    V.wait_ge(s_v.sem, guard_tt)
                ii = V.tensor_copy(bias2[nbuf][:, cch * 512:(cch + 1) * 512], psB1[:])
                bias_ev[0] = s_v.inc(ii)
                return bias_ev[0]

            for cch in range(4):
                bias_built[0] = build_bias_chunk(0, cch, 0)

            for i in range(NIT + 3):
                if i < NIT:
                    hh, j = iters[i]
                    L = LJ(i)
                    buf = hh % 2
                    if j == 0:
                        T.wait_ge(s_v.sem, bias_built[hh])
                        T.wait_ge(s_sl.sem, head_ld[hh])
                    if i >= 1:
                        T.wait_ge(s_v.sem, tt_mark[i - 1])
                    lastbank = ((L - 1) // 512) * 512
                    for c0 in range(0, L, 512):
                        w = min(512, L - c0)
                        ii = T.matmul(psS[:, c0:c0 + w],
                                      qt[buf][:, j * 128:(j + 1) * 128],
                                      kt[buf][:, c0:c0 + w],
                                      start=True, stop=(c0 != lastbank))
                    ii = T.matmul(psS[:, L - 128:L], trit_b[:], idsb[:],
                                  start=False, stop=True)
                    sc_mark[i] = s_t.inc(ii)
                    if j == 3 and hh + 1 < NH:
                        head_ld[hh + 1] = load_head(hh + 1, last_pv[hh - 1] if hh >= 1 else 0)
                    if 4 <= j <= 7 and hh + 1 < NH:
                        bias_built[hh + 1] = build_bias_chunk(
                            hh + 1, j - 4, tt_mark[i] if j == 4 else 0)
                if i >= 2 and i - 2 < NIT:
                    k = i - 2
                    nck2 = LJ(k) // 128
                    T.wait_ge(s_s.sem, ex_mark[k])
                    if k >= 1:
                        T.wait_ge(s_s.sem, cp_mark[k - 1])
                    for c in range(nck2):
                        ii = T.transpose(psT[:, c * 128:(c + 1) * 128],
                                         pb2[k % 2][:, c * 128:(c + 1) * 128], idsb[:])
                    tpm[k] = s_t.inc(ii)
                if i >= 3 and i - 3 < NIT:
                    k = i - 3
                    hh3, j3 = iters[k]
                    nck3 = LJ(k) // 128
                    slot = k % 2
                    T.wait_ge(s_s.sem, cp_mark[k])
                    if k >= 2:
                        T.wait_ge(s_s.sem, ev_mark[k - 2])
                    for c in range(nck3):
                        ii = T.matmul(psV[:, slot * 128:(slot + 1) * 128],
                                      ptb[:, c * 128:(c + 1) * 128],
                                      vh[hh3 % 2][:, c, :],
                                      start=(c == 0), stop=(c == nck3 - 1))
                    pv_mark[k] = s_t.inc(ii)
                    if j3 == NT_OWN - 1:
                        last_pv[hh3] = pv_mark[k]
                if i >= 1 and i - 1 < NIT:
                    k = i - 1
                    Lk = LJ(k)
                    SC.wait_ge(s_v.sem, mx_mark[k])
                    if k >= 2:
                        SC.wait_ge(s_t.sem, tpm[k - 2])
                    ii = SC.activation(pb2[k % 2][:, 0:Lk], ssb2[k % 2][:, 0:Lk], AF.Exp,
                                       bias=negm2[k % 2][:], scale=1.0,
                                       accum_out=lsum2[k % 2][:])
                    ex_mark[k] = s_s.inc(ii)
                if i >= 2 and i - 2 < NIT:
                    k = i - 2
                    nckk = LJ(k) // 128
                    SC.wait_ge(s_t.sem, tpm[k])
                    if k >= 1:
                        SC.wait_ge(s_t.sem, pv_mark[k - 1])
                    ii = SC.copy(ptb[:, 0:nckk * 128], psT[:, 0:nckk * 128])
                    cp_mark[k] = s_s.inc(ii)
                if i >= 3 and i - 3 < NIT:
                    k = i - 3
                    hk, jk = iters[k]
                    slot = k % 2
                    SC.wait_ge(s_t.sem, pv_mark[k])
                    SC.wait_ge(s_v.sem, rl_mark[k])
                    ii = SC.activation(ao[:, jk, hk * 128:(hk + 1) * 128],
                                       psV[:, slot * 128:(slot + 1) * 128],
                                       AF.Identity, bias=0.0, scale=rl4[k % 4][:])
                    ev_mark[k] = s_s.inc(ii)
                if i < NIT:
                    hh, j = iters[i]
                    L = LJ(i)
                    V.wait_ge(s_t.sem, sc_mark[i])
                    if i >= 2:
                        V.wait_ge(s_s.sem, ex_mark[i - 2])
                    ii = V.tensor_add(ssb2[i % 2][:, 0:L], psS[:, 0:L], bias2[hh % 2][:, 0:L])
                    tt_mark[i] = s_v.inc(ii)
                    V.wait_ge(s_v.sem, tt_mark[i])
                    ii = V.reduce_max(negm2[i % 2][:], ssb2[i % 2][:, 0:L],
                                      axis=AX.X, negate=True)
                    mx_mark[i] = s_v.inc(ii)
                if i >= 1 and i - 1 < NIT:
                    k = i - 1
                    V.wait_ge(s_s.sem, ex_mark[k])
                    ii = V.reciprocal(rl4[k % 4][:], lsum2[k % 2][:])
                    rl_mark[k] = s_v.inc(ii)
            if dbg:
                SC.wait_ge(s_v.sem, s_v.n)
                SC.wait_ge(s_s.sem, s_s.n)
                SC.wait_ge(s_t.sem, s_t.n)
                G.wait_ge(s_s.sem, s_s.n)
                G.wait_ge(s_v.sem, s_v.n)
                s_gl.dma(G.dma_start(out=dbg_p2[:, 0:S], in_=pb2[(NIT - 1) % 2][:]))
                s_cs.dma(SC.dma_start(out=dbg_p2[:, S:S + 1], in_=lsum2[(NIT - 1) % 2][:], allow_slow_non_contiguous=True))
                s_cs.dma(SC.dma_start(out=dbg_p2[:, S + 1:S + 2], in_=rl4[(NIT - 1) % 4][:], allow_slow_non_contiguous=True))
                s_cs.dma(SC.dma_start(out=dbg_p2[:, S + 2:S + 3], in_=negm2[(NIT - 1) % 2][:], allow_slow_non_contiguous=True))
                for j in range(NT_OWN):
                    s_cs.dma(SC.dma_start(out=dbg_ao[j * 128:(j + 1) * 128, :], in_=ao[:, j, :]))
            barrier()

        if phases < 5:
            return early_return()

        es3 = ExitStack()
        live_stacks.append(es3)
        aot = es3.enter_context(nc.sbuf_tensor("aot", [128, ND, HALF], BF16, side="right"))
        with ExitStack() as ctx:
            pst = ctx.enter_context(nc.psum_tensor("p4pst", [128, ND * 128], BF16))
            for j in range(NT_OWN):
                if j >= 1:
                    T.wait_ge(s_v.sem, s_v.n)
                for dch in range(ND):
                    ii = T.transpose(pst[:, dch * 128:(dch + 1) * 128],
                                     ao[:, j, dch * 128:(dch + 1) * 128], idsb[:])
                m = s_t.inc(ii)
                V.wait_ge(s_t.sem, m)
                ii = V.tensor_copy(aot[:, :, j * 128:(j + 1) * 128],
                                   pst[:].rearrange("p (a c) -> p a c", c=128))
                s_v.inc(ii)
            barrier()
        es2.close()
        live_stacks.remove(es2)

        with ExitStack() as ctx:
            wcol = [ctx.enter_context(nc.sbuf_tensor(f"p4w{i}", [128, ND, 512], BF16)) for i in range(2)]
            ev = [ctx.enter_context(nc.sbuf_tensor(f"p4ev{i}", [128, 512], F32)) for i in range(4)]
            psA = ctx.enter_context(nc.psum_tensor("p4psA", [128, 2048], F32))
            wout_r = wout.rearrange("(c p) o -> p c o", p=128)
            nsw = D // 512
            base_t = s_t.n
            fin_mark = [base_t + (oc + 1) * NT_OWN for oc in range(nsw)]
            wld = []
            for oc in range(nsw):
                buf = oc % 2
                if oc >= 2:
                    G.wait_ge(s_t.sem, fin_mark[oc - 2])
                wld.append(s_gl.dma(G.dma_start(out=wcol[buf][:],
                                                in_=wout_r[:, :, oc * 512:(oc + 1) * 512])))
            ev_done = {}
            st_done = {}
            for oc in range(nsw):
                buf = oc % 2
                T.wait_ge(s_gl.sem, wld[oc])
                for t in range(NT_OWN):
                    bank = (oc * NT_OWN + t) % 4
                    if bank in ev_done:
                        T.wait_ge(s_v.sem, ev_done[bank])
                    for d in range(ND):
                        ii = T.matmul(psA[:, bank * 512:(bank + 1) * 512],
                                      aot[:, d, t * 128:(t + 1) * 128],
                                      wcol[buf][:, d, :],
                                      start=(d == 0), stop=(d == ND - 1))
                    mm = s_t.inc(ii)
                    V.wait_ge(s_t.sem, mm)
                    V.wait_ge(s_cs.sem, st_done.get(bank, 0))
                    ii = V.tensor_copy(ev[bank][:], psA[:, bank * 512:(bank + 1) * 512])
                    ev_done[bank] = s_v.inc(ii)
                    SC.wait_ge(s_v.sem, ev_done[bank])
                    st_done[bank] = s_cs.dma(
                        SC.dma_start(out=proj_s[t * 128:(t + 1) * 128, oc * 512:(oc + 1) * 512],
                                     in_=ev[bank][:]))
            barrier()
        es3.close()
        live_stacks.remove(es3)

        if phases < 6:
            return early_return()

        es4 = ExitStack()
        live_stacks.append(es4)
        x2t = es4.enter_context(nc.sbuf_tensor("x2t", [128, ND, HALF], BF16))
        with ExitStack() as ctx:
            pr = [ctx.enter_context(nc.sbuf_tensor(f"p5pr{i}", [128, D], F32)) for i in range(2)]
            xt_ = [ctx.enter_context(nc.sbuf_tensor(f"p5xt{i}", [128, D], F32)) for i in range(2)]
            x2b = [ctx.enter_context(nc.sbuf_tensor(f"p5x2{i}", [128, D], BF16)) for i in range(2)]
            ws_a = stats_ws(ctx, "p5a")
            ws_b = stats_ws(ctx, "p5b")
            pst = ctx.enter_context(nc.psum_tensor("p5pst", [128, ND * 128], BF16))

            tp_t = [0] * NT_OWN
            ev_t = [0] * NT_OWN
            st_mark = [0] * NT_OWN
            na_m = [0] * NT_OWN
            nb_m = [0] * NT_OWN
            for t in range(NT_OWN):
                buf = t % 2
                if t >= 2:
                    SY.wait_ge(s_t.sem, tp_t[t - 2])
                    SY.wait_ge(s_cs.sem, st_mark[t - 2])
                ld = s_sl.dma(SY.dma_start(out=pr[buf][:], in_=proj_s[t * 128:(t + 1) * 128, :]))
                ld = s_sl.dma(SY.dma_start(out=xt_[buf][:], in_=x_seq[t * 128:(t + 1) * 128, :]))
                V.wait_ge(s_sl.sem, ld)
                m = emit_stats(pr[buf], ws_a, sc_guard=na_m[t - 1] if t >= 1 else 0)
                SC.wait_ge(s_v.sem, m)
                ii = SC.activation(pr[buf][:], pr[buf][:], AF.Identity,
                                   bias=ws_a[6][:], scale=ws_a[4][:])
                na_m[t] = s_s.inc(ii)
                V.wait_ge(s_s.sem, na_m[t])
                ii = V.tensor_add(xt_[buf][:], xt_[buf][:], pr[buf][:])
                xp_m = s_v.inc(ii)
                SC.wait_ge(s_v.sem, xp_m)
                st_mark[t] = s_cs.dma(SC.dma_start(out=xp_s[t * 128:(t + 1) * 128, :], in_=xt_[buf][:]))
                m = emit_stats(xt_[buf], ws_b, sc_guard=nb_m[t - 1] if t >= 1 else 0)
                SC.wait_ge(s_v.sem, m)
                ii = SC.activation(x2b[buf][:], xt_[buf][:], AF.Identity,
                                   bias=ws_b[6][:], scale=ws_b[4][:])
                nb_m[t] = s_s.inc(ii)
                T.wait_ge(s_s.sem, nb_m[t])
                if t >= 1:
                    T.wait_ge(s_v.sem, ev_t[t - 1])
                for dch in range(ND):
                    ii = T.transpose(pst[:, dch * 128:(dch + 1) * 128],
                                     x2b[buf][:, dch * 128:(dch + 1) * 128], idsb[:])
                tp_t[t] = s_t.inc(ii)
                V.wait_ge(s_t.sem, tp_t[t])
                ii = V.tensor_copy(x2t[:, :, t * 128:(t + 1) * 128],
                                   pst[:].rearrange("p (a c) -> p a c", c=128))
                ev_t[t] = s_v.inc(ii)
            barrier()

        if phases < 7:
            return early_return()

        with ExitStack() as ctx:
            gw = ctx.enter_context(nc.sbuf_tensor("p6gw", [128, ND, NE], BF16))
            lg = ctx.enter_context(nc.sbuf_tensor("p6lg", [128, NE], F32))
            pw = ctx.enter_context(nc.sbuf_tensor("p6pw", [128, 7, 7], F32))
            mk = ctx.enter_context(nc.sbuf_tensor("p6mk", [128, NE - 1], F32))
            psA = ctx.enter_context(nc.psum_tensor("p6psA", [128, 512], F32))
            s_gl.dma(G.dma_start(out=gw[:], in_=wgate.rearrange("(c p) o -> p c o", p=128)))
            T.wait_ge(s_gl.sem, s_gl.n)
            for t in range(NT_OWN):
                if t >= 1:
                    T.wait_ge(s_v.sem, s_v.n)
                for d in range(ND):
                    ii = T.matmul(psA[:, 0:NE], x2t[:, d, t * 128:(t + 1) * 128], gw[:, d, :],
                                  start=(d == 0), stop=(d == ND - 1))
                m = s_t.inc(ii)
                V.wait_ge(s_t.sem, m)
                ii = V.tensor_copy(lg[:], psA[:, 0:NE])
                s_v.inc(ii)
                V.wait_ge(s_v.sem, s_v.n)
                for kk in range(1, NE):
                    ii = V.tensor_add(pw[:, kk - 1, 0:NE - kk], lg[:, 0:NE - kk], lg[:, kk:NE])
                    s_v.inc(ii)
                V.wait_ge(s_v.sem, s_v.n)
                for kk in range(1, NE):
                    ii = V.reduce_max(mk[:, kk - 1:kk], pw[:, kk - 1, 0:NE - kk], axis=AX.X)
                    s_v.inc(ii)
                V.wait_ge(s_v.sem, s_v.n)
                ii = V.reduce_max(s2_all[:, t:t + 1], mk[:], axis=AX.X)
                s_v.inc(ii)
            barrier()

        es5 = ExitStack()
        live_stacks.append(es5)
        xlat = es5.enter_context(nc.sbuf_tensor("xlat", [128, NL, HALF], BF16, side="right"))
        with ExitStack() as ctx:
            wt_ = [ctx.enter_context(nc.sbuf_tensor(f"p7w{i}", [128, ND, 128], BF16)) for i in range(2)]
            psA = ctx.enter_context(nc.psum_tensor("p7psA", [128, 2048], F32))
            wfc1_r = wfc1.rearrange("(c p) o -> p c o", p=128)
            base_t = s_t.n
            fin_mark = [base_t + (lc + 1) * 2 for lc in range(NL)]
            wld = []
            for lc in range(NL):
                buf = lc % 2
                if lc >= 2:
                    G.wait_ge(s_t.sem, fin_mark[lc - 2])
                wld.append(s_gl.dma(G.dma_start(out=wt_[buf][:],
                                                in_=wfc1_r[:, :, lc * 128:(lc + 1) * 128])))
            evd = {}
            for lc in range(NL):
                buf = lc % 2
                T.wait_ge(s_gl.sem, wld[lc])
                for tb in range(2):
                    bank = (lc * 2 + tb) % 4
                    if bank in evd:
                        T.wait_ge(s_v.sem, evd[bank])
                    for d in range(ND):
                        ii = T.matmul(psA[:, bank * 512:(bank + 1) * 512],
                                      wt_[buf][:, d, :],
                                      x2t[:, d, tb * 512:(tb + 1) * 512],
                                      start=(d == 0), stop=(d == ND - 1))
                    mm = s_t.inc(ii)
                    V.wait_ge(s_t.sem, mm)
                    ii = V.tensor_copy(xlat[:, lc, tb * 512:(tb + 1) * 512],
                                       psA[:, bank * 512:(bank + 1) * 512])
                    evd[bank] = s_v.inc(ii)
            barrier()

        es6 = ExitStack()
        live_stacks.append(es6)
        eo = es6.enter_context(nc.sbuf_tensor("eo", [128, NL, HALF], BF16, side="right"))
        with ExitStack() as ctx:
            wt_ = [ctx.enter_context(nc.sbuf_tensor(f"p8w{i}", [128, NL, 128], BF16)) for i in range(2)]
            psA = ctx.enter_context(nc.psum_tensor("p8psA", [128, 2048], F32))
            wexp_r = wexp.rearrange("(c p) o -> p c o", p=128)
            base_t = s_t.n
            fin_mark = [base_t + (mc + 1) * 2 for mc in range(NL)]
            wld = []
            for mc in range(NL):
                buf = mc % 2
                if mc >= 2:
                    G.wait_ge(s_t.sem, fin_mark[mc - 2])
                wld.append(s_gl.dma(G.dma_start(out=wt_[buf][:],
                                                in_=wexp_r[:, :, mc * 128:(mc + 1) * 128])))
            evd = {}
            for mc in range(NL):
                buf = mc % 2
                T.wait_ge(s_gl.sem, wld[mc])
                for tb in range(2):
                    bank = (mc * 2 + tb) % 4
                    if bank in evd:
                        T.wait_ge(s_v.sem, evd[bank])
                    for lc in range(NL):
                        ii = T.matmul(psA[:, bank * 512:(bank + 1) * 512],
                                      wt_[buf][:, lc, :],
                                      xlat[:, lc, tb * 512:(tb + 1) * 512],
                                      start=(lc == 0), stop=(lc == NL - 1))
                    mm = s_t.inc(ii)
                    V.wait_ge(s_t.sem, mm)
                    ii = V.tensor_copy(eo[:, mc, tb * 512:(tb + 1) * 512],
                                       psA[:, bank * 512:(bank + 1) * 512])
                    evd[bank] = s_v.inc(ii)
            SC.wait_ge(s_v.sem, s_v.n)
            s_cs.dma(SC.dma_start(out=eo_s.rearrange("a p c -> p a c"), in_=eo[:]))
            barrier()
        es6.close()
        live_stacks.remove(es6)
        es5.close()
        live_stacks.remove(es5)

        if phases < 8:
            return early_return()

        with ExitStack() as ctx:
            wt_ = [ctx.enter_context(nc.sbuf_tensor(f"p9w{i}", [128, ND, 128], BF16)) for i in range(2)]
            htmp = [ctx.enter_context(nc.sbuf_tensor(f"p9h{i}", [128, 512], F32)) for i in range(2)]
            hb = [ctx.enter_context(nc.sbuf_tensor(f"p9hb{i}", [128, 512], BF16)) for i in range(4)]
            psA = ctx.enter_context(nc.psum_tensor("p9psA", [128, 2048], F32))
            wup_r = wup.rearrange("(c p) o -> p c o", p=128)
            base_t = s_t.n
            fin_mark = [base_t + (ic + 1) * 2 for ic in range(NI)]
            wld = []
            for ic in range(NI):
                buf = ic % 2
                if ic >= 2:
                    G.wait_ge(s_t.sem, fin_mark[ic - 2])
                wld.append(s_gl.dma(G.dma_start(out=wt_[buf][:],
                                                in_=wup_r[:, :, ic * 128:(ic + 1) * 128])))
            evd = {}
            st_done = {}
            for ic in range(NI):
                buf = ic % 2
                T.wait_ge(s_gl.sem, wld[ic])
                for tb in range(2):
                    it2 = ic * 2 + tb
                    bank = it2 % 4
                    if bank in evd:
                        T.wait_ge(s_s.sem, evd[bank])
                    for d in range(ND):
                        ii = T.matmul(psA[:, bank * 512:(bank + 1) * 512],
                                      wt_[buf][:, d, :],
                                      x2t[:, d, tb * 512:(tb + 1) * 512],
                                      start=(d == 0), stop=(d == ND - 1))
                    mm = s_t.inc(ii)
                    ebuf = it2 % 4
                    SC.wait_ge(s_t.sem, mm)
                    SC.wait_ge(s_cs.sem, st_done.get(ebuf, 0))
                    ii = SC.activation(htmp[it2 % 2][:], psA[:, bank * 512:(bank + 1) * 512], AF.Relu)
                    evd[bank] = s_s.inc(ii)
                    SC.wait_ge(s_s.sem, evd[bank])
                    ii = SC.activation(hb[ebuf][:], htmp[it2 % 2][:], AF.Square)
                    m = s_s.inc(ii)
                    SC.wait_ge(s_s.sem, m)
                    st_done[ebuf] = s_cs.dma(
                        SC.dma_start(out=h_s[ic, :, tb * 512:(tb + 1) * 512], in_=hb[ebuf][:]))
            barrier()
        es4.close()
        live_stacks.remove(es4)

        if phases < 9:
            return early_return()

        with ExitStack() as ctx:
            hh_ = ctx.enter_context(nc.sbuf_tensor("p10hh", [128, NI, 512], BF16))
            wcol = [ctx.enter_context(nc.sbuf_tensor(f"p10w{i}", [128, NI, 512], BF16)) for i in range(2)]
            ev = [ctx.enter_context(nc.sbuf_tensor(f"p10ev{i}", [128, 512], F32)) for i in range(4)]
            psA = ctx.enter_context(nc.psum_tensor("p10psA", [128, 2048], F32))
            wdown_r = wdown.rearrange("(c p) o -> p c o", p=128)
            h_r = h_s.rearrange("a p c -> p a c")
            for th in range(2):
                SY.wait_ge(s_t.sem, s_t.n)
                h_ld = s_sl.dma(SY.dma_start(out=hh_[:], in_=h_r[:, :, th * 512:(th + 1) * 512]))
                base_t = s_t.n
                fin_mark = [base_t + (oc + 1) * 4 for oc in range(8)]
                wld = []
                for oc in range(8):
                    buf = oc % 2
                    if oc >= 2:
                        G.wait_ge(s_t.sem, fin_mark[oc - 2])
                    wld.append(s_gl.dma(G.dma_start(out=wcol[buf][:],
                                                    in_=wdown_r[:, :, oc * 512:(oc + 1) * 512])))
                evd = {}
                st_done = {}
                for oc in range(8):
                    buf = oc % 2
                    T.wait_ge(s_gl.sem, wld[oc])
                    T.wait_ge(s_sl.sem, h_ld)
                    for t in range(4):
                        bank = (oc * 4 + t) % 4
                        if bank in evd:
                            T.wait_ge(s_v.sem, evd[bank])
                        for i in range(NI):
                            ii = T.matmul(psA[:, bank * 512:(bank + 1) * 512],
                                          hh_[:, i, t * 128:(t + 1) * 128],
                                          wcol[buf][:, i, :],
                                          start=(i == 0), stop=(i == NI - 1))
                        mm = s_t.inc(ii)
                        V.wait_ge(s_t.sem, mm)
                        V.wait_ge(s_cs.sem, st_done.get(bank, 0))
                        ii = V.tensor_copy(ev[bank][:], psA[:, bank * 512:(bank + 1) * 512])
                        evd[bank] = s_v.inc(ii)
                        SC.wait_ge(s_v.sem, evd[bank])
                        r0 = th * 512 + t * 128
                        st_done[bank] = s_cs.dma(
                            SC.dma_start(out=sh_s[r0:r0 + 128, oc * 512:(oc + 1) * 512],
                                         in_=ev[bank][:]))
            barrier()

        if phases < 10:
            return early_return()

        with ExitStack() as ctx:
            wf2 = ctx.enter_context(nc.sbuf_tensor("p11w", [128, NL, D], BF16))
            eot = [ctx.enter_context(nc.sbuf_tensor(f"p11e{i}", [128, NL, 128], BF16)) for i in range(2)]
            shx = ctx.enter_context(nc.sbuf_tensor("p11sx", [128, D], F32))
            z = ctx.enter_context(nc.sbuf_tensor("p11z", [128, D], F32))
            gb = ctx.enter_context(nc.sbuf_tensor("p11gb", [128, D], F32))
            bb = ctx.enter_context(nc.sbuf_tensor("p11bb", [128, D], F32))
            vec1 = ctx.enter_context(nc.sbuf_tensor("p11vec", [1, 512], F32))
            ws_a = stats_ws(ctx, "p11a")
            ws_b = stats_ws(ctx, "p11b")
            psA = ctx.enter_context(nc.psum_tensor("p11psA", [128, 2048], F32))
            psB = ctx.enter_context(nc.psum_tensor("p11psB", [128, 2048], F32))

            s_gl.dma(G.dma_start(out=wf2[:], in_=wfc2.rearrange("(c p) o -> p c o", p=128)))

            for wi, vecap in enumerate((gamma, beta)):
                dstt = gb if wi == 0 else bb
                for cch in range(8):
                    T.wait_ge(s_v.sem, s_v.n)
                    SY.wait_ge(s_t.sem, s_t.n)
                    ld = s_sl.dma(SY.dma_start(out=vec1[:], in_=vecap[0:1, cch * 512:(cch + 1) * 512]))
                    T.wait_ge(s_sl.sem, ld)
                    ii = T.matmul(psA[:, 0:512], ones1f[:], vec1[:], start=True, stop=True)
                    m = s_t.inc(ii)
                    V.wait_ge(s_t.sem, m)
                    ii = V.tensor_copy(dstt[:, cch * 512:(cch + 1) * 512], psA[:, 0:512])
                    s_v.inc(ii)

            T.wait_ge(s_gl.sem, s_gl.n)
            zev = 0
            mm_t = [0] * NT_OWN
            fin_t = [0] * NT_OWN
            na_m = [0] * NT_OWN
            nb_m = [0] * NT_OWN
            eld = [0] * NT_OWN
            for t in range(NT_OWN):
                buf = t % 2
                if t >= 2:
                    SY.wait_ge(s_t.sem, mm_t[t - 2])
                eld[t] = s_sl.dma(SY.dma_start(out=eot[buf][:],
                                      in_=eo_s.rearrange("a p c -> p a c")[:, :, t * 128:(t + 1) * 128]))
                if t >= 1:
                    SY.wait_ge(s_v.sem, fin_t[t - 1])
                ld = s_sl.dma(SY.dma_start(out=shx[:], in_=sh_s[t * 128:(t + 1) * 128, :]))
                T.wait_ge(s_sl.sem, eld[t])
                if zev:
                    T.wait_ge(s_v.sem, zev)
                for mc in range(NL):
                    for o in range(8):
                        pbank = psA if o < 4 else psB
                        ii = T.matmul(pbank[:, (o % 4) * 512:((o % 4) + 1) * 512],
                                      eot[buf][:, mc, :],
                                      wf2[:, mc, o * 512:(o + 1) * 512],
                                      start=(mc == 0), stop=(mc == NL - 1))
                mm = s_t.inc(ii)
                mm_t[t] = mm
                V.wait_ge(s_t.sem, mm)
                V.wait_ge(s_sl.sem, ld)
                for o in range(8):
                    pbank = psA if o < 4 else psB
                    ii = V.tensor_scalar(z[:, o * 512:(o + 1) * 512],
                                         pbank[:, (o % 4) * 512:((o % 4) + 1) * 512],
                                         s2_all[:, t:t + 1], None, ALU.mult)
                    zev = s_v.inc(ii)
                V.wait_ge(s_v.sem, zev)
                ii = V.tensor_add(z[:], z[:], shx[:])
                m_shc = s_v.inc(ii)
                SY.wait_ge(s_v.sem, m_shc)
                ld2 = s_sl.dma(SY.dma_start(out=shx[:], in_=xp_s[t * 128:(t + 1) * 128, :]))
                m = emit_stats(z, ws_a, sc_guard=na_m[t - 1] if t >= 1 else 0)
                SC.wait_ge(s_v.sem, m)
                ii = SC.activation(z[:], z[:], AF.Identity, bias=ws_a[6][:], scale=ws_a[4][:])
                na_m[t] = s_s.inc(ii)
                V.wait_ge(s_s.sem, na_m[t])
                ii = V.tensor_mul(z[:], z[:], gb[:])
                s_v.inc(ii)
                V.wait_ge(s_v.sem, s_v.n)
                ii = V.tensor_add(z[:], z[:], bb[:])
                s_v.inc(ii)
                m = emit_stats(z, ws_b, sc_guard=nb_m[t - 1] if t >= 1 else 0)
                SC.wait_ge(s_v.sem, m)
                ii = SC.activation(z[:], z[:], AF.Identity, bias=ws_b[6][:], scale=ws_b[4][:])
                nb_m[t] = s_s.inc(ii)
                V.wait_ge(s_s.sem, nb_m[t])
                V.wait_ge(s_sl.sem, ld2)
                ii = V.tensor_add(z[:], z[:], shx[:])
                fin = s_v.inc(ii)
                fin_t[t] = fin
                SC.wait_ge(s_v.sem, fin)
                st = s_cs.dma(SC.dma_start(out=out[t * 128:(t + 1) * 128, :], in_=z[:]))
                V.wait_ge(s_cs.sem, st)

            G.wait_ge(s_cs.sem, s_cs.n)
            G.wait_ge(s_gl.sem, s_gl.n)
            barrier()

    return nc


def _host_inputs(inputs):
    x = np.asarray(inputs["L0_attn_hidden_states"], dtype=np.float32)
    Wqkv = np.asarray(inputs["L0_attn_Wqkv"], dtype=np.float32)
    Wout = np.asarray(inputs["L0_attn_Wout"], dtype=np.float32)

    shared = {
        "wq": np.ascontiguousarray(Wqkv[0:D, :].T),
        "wkv": np.ascontiguousarray(Wqkv[D:3 * D, :].T),
        "wout": np.ascontiguousarray(Wout.T),
        "wup": np.ascontiguousarray(np.asarray(inputs["L0_ffn_up_w"], dtype=np.float32)),
        "wdown": np.ascontiguousarray(np.asarray(inputs["L0_ffn_down_w"], dtype=np.float32)),
        "wfc1": np.ascontiguousarray(np.asarray(inputs["L0_ffn_fc1_w"], dtype=np.float32)),
        "wfc2": np.ascontiguousarray(np.asarray(inputs["L0_ffn_fc2_w"], dtype=np.float32)),
        "wexp": np.ascontiguousarray(np.asarray(inputs["L0_ffn_expert_w"], dtype=np.float32)),
        "wgate": np.ascontiguousarray(np.asarray(inputs["L0_ffn_gate_w"], dtype=np.float32)),
        "qlnw": np.ascontiguousarray(np.asarray(inputs["L0_attn_q_ln_w"], dtype=np.float32).reshape(1, D)),
        "qlnb": np.ascontiguousarray(np.asarray(inputs["L0_attn_q_ln_b"], dtype=np.float32).reshape(1, D)),
        "klnw": np.ascontiguousarray(np.asarray(inputs["L0_attn_k_ln_w"], dtype=np.float32).reshape(1, D)),
        "klnb": np.ascontiguousarray(np.asarray(inputs["L0_attn_k_ln_b"], dtype=np.float32).reshape(1, D)),
        "gamma": np.ascontiguousarray(np.asarray(inputs["L0_ffn_ln_gamma"], dtype=np.float32).reshape(1, D)),
        "beta": np.ascontiguousarray(np.asarray(inputs["L0_ffn_ln_beta"], dtype=np.float32).reshape(1, D)),
        "tri": np.where(np.arange(128)[None, :] <= np.arange(128)[:, None], 0.0, NEG).astype(np.float32),
        "trit": np.ascontiguousarray(np.where(np.arange(128)[None, :] <= np.arange(128)[:, None], 0.0, NEG).astype(np.float32).T),
        "ident": np.eye(128, dtype=np.float32),
    }

    slopes = _alibi_slopes()
    in_maps = []
    for c in range(8):
        b, half = c // 2, c % 2
        own = x[b, half * HALF:(half + 1) * HALF, :]
        peer = x[b, (1 - half) * HALF:(2 - half) * HALF, :]
        x_seq_c = np.ascontiguousarray(np.concatenate([own, peer], axis=0))
        kg = np.concatenate([
            (1 - half) * HALF + np.arange(HALF),
            half * HALF + np.arange(HALF),
        ]).astype(np.float64)
        kps_c = (slopes[:, None] * kg[None, :]).astype(np.float32)
        if half == 0:
            kps_c[:, 0:HALF] = NEG
        kps_c = np.ascontiguousarray(kps_c)
        maska_c = np.full((1, HALF), NEG if half == 0 else 0.0, dtype=np.float32)
        im = dict(shared)
        im["x_seq"] = x_seq_c
        im["kps"] = kps_c
        im["maska"] = maska_c
        in_maps.append(im)
    return in_maps


_nc_cache = {}


def kernel(**inputs):
    if "full" not in _nc_cache:
        _nc_cache["full"] = build_nc()
    nc = _nc_cache["full"]
    in_maps = _host_inputs(inputs)
    res = run_bass_kernel_spmd(nc, in_maps, core_ids=list(range(8)))
    outp = np.empty((B, S, D), dtype=np.float32)
    for c in range(8):
        b, half = c // 2, c % 2
        outp[b, half * HALF:(half + 1) * HALF, :] = res.results[c]["out"]
    return outp
